# version 04
# baseline (speedup 1.0000x reference)
import numpy as np
import ml_dtypes
import concourse.bass as bass
import concourse.mybir as mybir
from concourse.bass_utils import run_bass_kernel_spmd

F32 = mybir.dt.float32
BF16 = mybir.dt.bfloat16
AF = mybir.ActivationFunctionType
ALU = mybir.AluOpType
AX = mybir.AxisListType

SEQ = 2048
DM = 1024
NH = 16
DH = 64
NT = 16
NCK = 8
ENGS = ("sync", "gpsimd", "vector", "scalar", "tensor")


class Chan:
    def __init__(self, nc, es, name):
        self.sem = es.enter_context(nc.semaphore(name))
        self.n = 0


class Sched:

    def __init__(self):
        self.ev = {k: [] for k in ENGS}

    strict = {}

    def op(self, eng, meth, *args, inc=None, k=1, **kw):
        strict_ch = None
        if meth != "dma_start" and eng in self.strict:
            strict_ch = self.strict[eng]
            if inc is None:
                inc = strict_ch
        sem = None
        if inc is not None:
            inc.n += k
            sem = inc.sem
        def g(e, meth=meth, args=args, kw=kw, sem=sem, k=k):
            ins = getattr(e, meth)(*args, **kw)
            if sem is not None:
                ins.then_inc(sem, k)
        self.ev[eng].append(g)
        if strict_ch is not None:
            self.wait(eng, strict_ch)

    def wait(self, eng, ch, n=None):
        tgt = ch.n if n is None else n
        if tgt <= 0:
            return
        self.ev[eng].append(lambda e, s=ch.sem, t=tgt: e.wait_ge(s, t))


def build_nc(stage="full"):
    nc = bass.Bass("TRN2", target_bir_lowering=False,
                   dynamic_dma_scratch_size=16384)

    def din(name, shape, dt=F32):
        return nc.dram_tensor(name, list(shape), dt, kind="ExternalInput").ap()

    x_h = din("x", [2, SEQ, DM])
    wq_h = din("w_q", [DM, DM])
    wkv_h = din("w_kv", [DM, 2 * DM])
    wout_h = din("w_out", [DM, DM])
    bout_h = din("b_out", [1, DM])
    scale_h = din("scale", [1, NH])
    cw1_h = din("cn_w1", [DM, DM])
    cw2_h = din("cn_w2", [DM, 512])
    cw3_h = din("cn_w3", [512, 256])
    cw4_h = din("cn_w4", [256, 1])
    gw1_h = din("gc_w1", [DM, 512])
    gw2_h = din("gc_w2", [512, 256])
    gw3_h = din("gc_w3", [256, 1])
    cb1_h = din("cn_b1", [8, 128, 1])
    cb2_h = din("cn_b2", [4, 128, 1])
    cb3_h = din("cn_b3", [2, 128, 1])
    gb1_h = din("gc_b1", [4, 128, 1])
    gb2_h = din("gc_b2", [2, 128, 1])
    cb4_h = din("cn_b4", [1, 1])
    gb3_h = din("gc_b3", [1, 1])
    ar_h = din("ar", [1, 1])
    onesf_h = din("onesf", [128, 128])
    identf_h = din("identf", [128, 128])
    identb_h = din("identb", [128, 128], BF16)
    onesb_h = din("onesb", [128, 128], BF16)
    out_h = nc.dram_tensor("out", [2, SEQ, DM], F32, kind="ExternalOutput").ap()

    from contextlib import ExitStack
    es = ExitStack()
    sb = lambda n, s, d=F32: es.enter_context(nc.sbuf_tensor(n, s, d))

    AR0 = sb("AR0", [128, 8192])
    AR1 = sb("AR1", [128, 8192])
    AR2 = sb("AR2", [128, 8192])
    AR3 = sb("AR3", [128, 8320])
    AR4 = sb("AR4", [128, 8192])
    W = sb("W", [128, 8192])
    SCRF = sb("SCRF", [128, 2048])
    ln1T = AR0.ap().bitcast(BF16)
    qT = AR1.ap().bitcast(BF16)
    kT = AR2.ap().bitcast(BF16)
    Vb = AR3.ap().bitcast(BF16)
    A4 = AR4.ap().bitcast(BF16)
    Wb = W.ap().bitcast(BF16)
    SCRB = SCRF.ap().bitcast(BF16)

    def xw_tile(j):
        return (AR0.ap() if j < 8 else AR1.ap())[:, (j % 8) * 1024:(j % 8 + 1) * 1024]

    def fout_tile(j):
        return (AR2.ap() if j < 8 else AR3.ap())[:, (j % 8) * 1024:(j % 8 + 1) * 1024]

    onesf = sb("c_onesf", [128, 128]).ap()
    identf = sb("c_identf", [128, 128]).ap()
    identb = sb("c_identb", [128, 128], BF16).ap()
    onesb = sb("c_onesb", [128, 128], BF16).ap()
    sc_row = sb("sc_row", [1, NH]).ap()
    eS_row = sb("eS_row", [1, NH]).ap()
    eS128 = sb("eS128", [128, NH]).ap()
    ar128 = sb("ar128", [128, 1]).ap()
    cnb4c = sb("cnb4c", [128, 1]).ap()
    A128 = sb("A128", [128, 1]).ap()
    B128 = sb("B128", [128, 1]).ap()
    gf128 = sb("gf128", [128, 1]).ap()
    u128 = sb("u128", [128, 1]).ap()
    bhi = sb("bhi", [1, DM], BF16).ap()
    cnb1 = sb("cnb1", [128, 8]).ap()
    cnb2 = sb("cnb2", [128, 4]).ap()
    cnb3 = sb("cnb3", [128, 2]).ap()
    gcb1 = sb("gcb1", [128, 4]).ap()
    gcb2 = sb("gcb2", [128, 2]).ap()
    gcb3 = sb("gcb3", [1, 1]).ap()
    ar_sb = sb("ar_sb", [1, 1]).ap()
    cnb4_sb = sb("cnb4_sb", [1, 1]).ap()
    c_eps = sb("c_eps", [128, 1]).ap()
    c_invdm = sb("c_invdm", [128, 1]).ap()
    c_invS = sb("c_invS", [128, 1]).ap()
    stm_t = sb("stm", [128, 2]).ap()
    sts_t = sb("sts", [128, 2]).ap()
    ststd_t = sb("ststd", [128, 2]).ap()
    strd_t = sb("strd", [128, 2]).ap()
    ssq16_t = sb("ssq16", [128, 2 * NH]).ap()
    n16_t = sb("n16", [128, 2 * NH]).ap()
    r16_t = sb("r16", [128, 2 * NH]).ap()
    rq16_t = sb("rq16", [128, 2 * NH]).ap()
    stm = lambda p: stm_t[:, p:p + 1]
    sts = lambda p: sts_t[:, p:p + 1]
    ststd = lambda p: ststd_t[:, p:p + 1]
    strd = lambda p: strd_t[:, p:p + 1]
    ssq16 = lambda p: ssq16_t[:, p * NH:(p + 1) * NH]
    n16 = lambda p: n16_t[:, p * NH:(p + 1) * NH]
    r16 = lambda p: r16_t[:, p * NH:(p + 1) * NH]
    rq16 = lambda p: rq16_t[:, p * NH:(p + 1) * NH]
    cn4b = sb("cn4b", [128, 2], BF16).ap()
    gc3b = sb("gc3b", [128, 2], BF16).ap()
    gcb = sb("gcb_t", [128, 8], BF16).ap()
    h1gb = sb("h1gb", [128, 4], BF16).ap()
    h2gb = sb("h2gb", [128, 2], BF16).ap()
    gf_s = sb("gf_s", [1, 1]).ap()
    tfs = sb("tfs", [128, NT]).ap()
    coup = sb("coup", [128, NT]).ap()
    ncoup = sb("ncoup", [128, NT]).ap()

    PSt = es.enter_context(nc.psum_tensor("PS", [128, 4096], F32))
    PS = PSt.ap()
    PSb = PS.bitcast(BF16)
    def bk(i, n=1):
        return PS[:, 512 * i:512 * (i + n)]
    def bkb(i):
        return PSb[:, 1024 * i:1024 * (i + 1)]

    lnt = lambda par: A4[:, par * 1024:(par + 1) * 1024]
    ao_v = lambda c: A4[:, c * 2048:(c + 1) * 2048]
    outTg = lambda c: A4[:, 2048 + c * 512:2048 + (c + 1) * 512]
    h1b = lambda o: A4[:, 6144 + o * 512:6144 + (o + 1) * 512]
    h2b = lambda o: A4[:, 10240 + o * 512:10240 + (o + 1) * 512]
    h3b = lambda o: A4[:, 12288 + o * 512:12288 + (o + 1) * 512]
    cn3v = lambda c: A4[:, 13312 + c * 256:13312 + (c + 1) * 256]
    gc2v = lambda c: A4[:, 14336 + c * 256:14336 + (c + 1) * 256]
    trash = A4[:, 6144:7168]
    xin = lambda par: SCRF.ap()[:, par * 1024:(par + 1) * 1024]
    qsq = lambda p: SCRF.ap()[:, p * 1024:(p + 1) * 1024]
    p_sb = lambda par: SCRB[:, par * 1024:(par + 1) * 1024]
    epi65 = SCRF.ap()[0:65, 1024:2048]
    tw0 = lambda par: SCRF.ap()[:, par * 512:par * 512 + 256]
    tw1 = lambda par: SCRF.ap()[:, par * 512 + 256:par * 512 + 512]
    stg = lambda par: SCRF.ap()[:, par * 1024:(par + 1) * 1024]
    vones = Vb.rearrange("p (t h e) -> p t h e", t=NT, h=NH)[:, :, :, 64:65]

    S = Sched()
    cL = Chan(nc, es, "cL")
    cW = Chan(nc, es, "cW")
    cS = Chan(nc, es, "cS")
    cT = Chan(nc, es, "cT")
    cA = Chan(nc, es, "cA")
    cV = Chan(nc, es, "cV")
    cG = Chan(nc, es, "cG")
    S.strict = {"vector": cV}

    def load_w(h_ap, bf16_col_off, nchunk, ncols):
        out = Wb[:, bf16_col_off:bf16_col_off + nchunk * ncols]
        S.op("gpsimd", "dma_start",
             out=out.rearrange("p (c n) -> p c n", c=nchunk),
             in_=h_ap.rearrange("(c p) n -> p c n", p=128), inc=cW, k=16)
        return cW.n

    wA = lambda c: Wb[:, c * 1024:(c + 1) * 1024]
    wOUT = lambda c: Wb[:, 8192 + c * 1024:8192 + (c + 1) * 1024]
    wCN2 = lambda c: Wb[:, 8192 + c * 512:8192 + (c + 1) * 512]
    wGC1 = lambda c: Wb[:, 12288 + c * 512:12288 + (c + 1) * 512]

    for dst, src in [(onesf, onesf_h), (identf, identf_h), (identb, identb_h),
                     (onesb, onesb_h), (sc_row, scale_h),
                     (ar_sb, ar_h), (cnb4_sb, cb4_h), (gcb3, gb3_h)]:
        S.op("sync", "dma_start", out=dst, in_=src, inc=cL, k=16)
    for dst, src, kk in [(cnb1, cb1_h, 8), (cnb2, cb2_h, 4), (cnb3, cb3_h, 2),
                         (gcb1, gb1_h, 4), (gcb2, gb2_h, 2)]:
        for c in range(kk):
            S.op("sync", "dma_start", out=dst[:, c:c + 1], in_=src[c],
                 inc=cL, k=16)
    S.op("gpsimd", "dma_start", out=bhi, in_=bout_h, inc=cW, k=16)
    S.op("gpsimd", "dma_start", out=cn4b.unsqueeze(2),
         in_=cw4_h.rearrange("(c p) n -> p c n", p=128), inc=cW, k=16,
         allow_slow_non_contiguous=True)
    S.op("gpsimd", "dma_start", out=gc3b.unsqueeze(2),
         in_=gw3_h.rearrange("(c p) n -> p c n", p=128), inc=cW, k=16,
         allow_slow_non_contiguous=True)
    smallw_setup = cW.n

    S.wait("scalar", cL)
    S.op("scalar", "activation", eS_row, sc_row, AF.Exp, inc=cA)
    S.wait("tensor", cA)
    S.wait("tensor", cL)
    S.op("tensor", "matmul", bk(6)[:, 0:16], onesf[0:1, 0:128], eS_row,
         start=True, stop=True)
    S.op("tensor", "matmul", bk(6)[:, 16:17], onesf[0:1, 0:128], ar_sb,
         start=True, stop=True)
    S.op("tensor", "matmul", bk(6)[:, 17:18], onesf[0:1, 0:128], cnb4_sb,
         start=True, stop=True, inc=cT)
    S.wait("vector", cT)
    S.op("vector", "memset", c_eps, 1e-5)
    S.op("vector", "memset", c_invdm, 1.0 / DM)
    S.op("vector", "memset", c_invS, 1.0 / SEQ)
    S.op("vector", "tensor_copy", eS128, bk(6)[:, 0:16])
    S.op("vector", "tensor_copy", ar128, bk(6)[:, 16:17])
    S.op("vector", "tensor_copy", cnb4c, bk(6)[:, 17:18], inc=cV)
    S.wait("vector", cV)
    S.op("vector", "tensor_scalar_mul", A128, ar128, 0.6)
    S.op("vector", "tensor_copy", A128, A128, inc=cV)
    setup_v = cV.n

    tpb_free = {0: 0, 1: 0}

    def transpose_tile_to(dst_arena, j, src_tile, par):
        tp = bkb(6 + par)
        S.wait("tensor", cV, cV.n)
        for c in range(NCK):
            S.op("tensor", "transpose", tp[:, c * 128:(c + 1) * 128],
                 src_tile[:, c * 128:(c + 1) * 128], identb,
                 inc=(cT if c == NCK - 1 else None))
        S.wait("vector", cT)
        dstv = dst_arena.rearrange("p (c n) -> p c n", c=NCK)[
            :, :, j * 128:(j + 1) * 128]
        S.op("vector", "tensor_copy", dstv,
             tp.rearrange("p (c n) -> p c n", c=NCK), inc=cV)
        tpb_free[par] = cV.n

    tpb_free[0] = tpb_free[1] = None

    base_stage, _, _sfx = stage.partition("@")
    dump_s = int(_sfx) if _sfx else 0

    def seq_program(s):
        st = base_stage if s == dump_s else "full"
        if tpb_free[0] is None:
            tpb_free[0] = tpb_free[1] = setup_v
        S.wait("sync", cS, cS.n)
        tile_done = {}
        for j in range(NT):
            par = j % 2
            if j >= 2:
                S.wait("sync", cV, tile_done[j - 2])
            S.op("sync", "dma_start", out=xin(par),
                 in_=x_h[s, j * 128:(j + 1) * 128, :], inc=cL, k=16)
            ld = cL.n
            S.wait("vector", cL, ld)
            S.op("vector", "reduce_sum", stm(par), xin(par), AX.X, inc=cV)
            S.wait("vector", cV)
            S.op("vector", "tensor_scalar_mul", stm(par), stm(par), 1.0 / DM,
                 inc=cV)
            S.wait("vector", cV)
            S.op("vector", "tensor_single_scalar", xin(par), xin(par), stm(par),
                 ALU.subtract, inc=cV)
            S.op("vector", "scalar_tensor_tensor", trash, xin(par), 1.0,
                 xin(par), ALU.mult, ALU.mult, accum_out=sts(par), inc=cV)
            S.wait("scalar", cV)
            S.op("scalar", "activation", ststd(par), sts(par), AF.Sqrt,
                 scale=c_invdm, bias=c_eps, inc=cA)
            S.wait("vector", cA)
            S.op("vector", "reciprocal", strd(par), ststd(par), inc=cV)
            S.wait("vector", cV)
            S.op("vector", "tensor_single_scalar", lnt(par), xin(par), strd(par),
                 ALU.mult, inc=cV)
            tile_done[j] = cV.n
            transpose_tile_to(ln1T, j, lnt(par), par)
        if st == "ln1T":
            done = cV.n
            for t in range(8):
                S.wait("sync", cV, done)
                S.op("sync", "dma_start", out=out_h[s, t * 128:(t + 1) * 128, :],
                     in_=AR0.ap()[:, t * 1024:(t + 1) * 1024], inc=cS, k=16)
            return True

        S.wait("gpsimd", cT, cT.n)
        wk_ld = load_w(wkv_h[:, 0:DM], 0, 8, 1024)
        wv_ld = load_w(wkv_h[:, DM:2 * DM], 8192, 8, 1024)
        S.wait("gpsimd", cV, cV.n)
        S.op("gpsimd", "memset", vones, 1.0, inc=cG)

        def proj_tile(j, wview, par):
            pj = bk(2 * par, 2)
            for half in range(2):
                for c in range(NCK):
                    S.op("tensor", "matmul",
                         pj[:, half * 512:(half + 1) * 512],
                         ln1T[:, c * 2048 + j * 128:c * 2048 + (j + 1) * 128],
                         wview(c)[:, half * 512:(half + 1) * 512],
                         start=(c == 0), stop=(c == NCK - 1),
                         inc=(cT if (c == NCK - 1) else None))
            return pj

        def l2norm_evac(pj, par, with_scale, qsq_guard):
            mm_done = cT.n
            S.wait("scalar", cT, mm_done)
            S.wait("scalar", cV, qsq_guard)
            S.op("scalar", "activation", qsq(par), pj, AF.Square, inc=cA)
            S.wait("vector", cA)
            S.op("vector", "tensor_reduce", ssq16(par),
                 qsq(par).rearrange("p (h e) -> p h e", e=DH), AX.X, ALU.add,
                 inc=cV)
            S.wait("vector", cV)
            S.op("vector", "tensor_scalar_max", ssq16(par), ssq16(par), 1e-24,
                 inc=cV)
            S.wait("scalar", cV)
            S.op("scalar", "activation", n16(par), ssq16(par), AF.Sqrt, inc=cA)
            S.wait("vector", cA)
            S.op("vector", "reciprocal", r16(par), n16(par), inc=cV)
            S.wait("vector", cV)
            mul = r16(par)
            if with_scale:
                S.op("vector", "tensor_mul", rq16(par), r16(par), eS128,
                     inc=cV)
                S.wait("vector", cV)
                mul = rq16(par)
            S.op("vector", "tensor_tensor",
                 lnt(par).rearrange("p (h e) -> p h e", e=DH),
                 pj.rearrange("p (h e) -> p h e", e=DH),
                 mul.unsqueeze(2).broadcast_to([128, NH, DH]),
                 ALU.mult, inc=cV)

        S.wait("tensor", cA, cA.n)
        S.wait("tensor", cW, wk_ld)
        pj_free = {0: cV.n, 1: cV.n}
        for j in range(NT):
            par = j % 2
            S.wait("tensor", cV, pj_free[par])
            pj = proj_tile(j, wA, par)
            l2norm_evac(pj, par, with_scale=False, qsq_guard=pj_free[par])
            pj_free[par] = cV.n
            transpose_tile_to(kT, j, lnt(par), par)
        if st == "lqk":
            done = cV.n
            for t in range(8):
                S.wait("sync", cV, done)
                S.op("sync", "dma_start", out=out_h[s, t * 128:(t + 1) * 128, :],
                     in_=AR0.ap()[:, t * 1024:(t + 1) * 1024], inc=cS, k=16)
                S.op("sync", "dma_start",
                     out=out_h[s, 1024 + t * 128:1024 + (t + 1) * 128, :],
                     in_=AR2.ap()[:, t * 1024:(t + 1) * 1024], inc=cS, k=16)
            return True
        S.wait("gpsimd", cT, cT.n)
        wq_ld = load_w(wq_h, 0, 8, 1024)
        S.wait("tensor", cW, wv_ld)
        S.wait("tensor", cG)
        for j in range(NT):
            par = j % 2
            S.wait("tensor", cV, pj_free[par])
            pj = proj_tile(j, wOUT, par)
            S.wait("vector", cT, cT.n)
            vt = Vb[:, j * 1040:(j + 1) * 1040].rearrange(
                "p (h e) -> p h e", e=65)[:, :, 0:64]
            S.op("vector", "tensor_copy", vt,
                 pj.rearrange("p (h e) -> p h e", e=DH), inc=cV)
            pj_free[par] = cV.n
        S.wait("tensor", cW, wq_ld)
        for j in range(NT):
            par = j % 2
            S.wait("tensor", cV, pj_free[par])
            pj = proj_tile(j, wA, par)
            l2norm_evac(pj, par, with_scale=True, qsq_guard=pj_free[par])
            pj_free[par] = cV.n
            transpose_tile_to(qT, j, lnt(par), par)

        if st == "lq":
            done = cV.n
            for t in range(8):
                S.wait("sync", cV, done)
                S.op("sync", "dma_start", out=out_h[s, t * 128:(t + 1) * 128, :],
                     in_=AR0.ap()[:, t * 1024:(t + 1) * 1024], inc=cS, k=16)
                S.op("sync", "dma_start",
                     out=out_h[s, 1024 + t * 128:1024 + (t + 1) * 128, :],
                     in_=AR1.ap()[:, t * 1024:(t + 1) * 1024], inc=cS, k=16)
            return True
        if st == "qkT":
            done = cV.n
            for t in range(8):
                S.wait("sync", cV, done)
                S.op("sync", "dma_start", out=out_h[s, t * 128:(t + 1) * 128, :],
                     in_=AR1.ap()[:, t * 1024:(t + 1) * 1024], inc=cS, k=16)
                S.op("sync", "dma_start",
                     out=out_h[s, 1024 + t * 128:1024 + (t + 1) * 128, :],
                     in_=AR2.ap()[:, t * 1024:(t + 1) * 1024], inc=cS, k=16)
            return True

        S.wait("gpsimd", cT, cT.n)
        wout_ld = load_w(wout_h, 8192, 8, 1024)
        cn1_ld = load_w(cw1_h, 0, 8, 1024)
        S.wait("tensor", cV, cV.n)
        S.wait("scalar", cV, cV.n)
        po_free = cV.n
        sim_free = {0: cA.n, 1: cA.n}
        p_free = {0: cT.n, 1: cT.n}
        for h in range(NH):
            c = h // 2
            poff = 64 * (h % 2)
            qTh = qT[:, c * 2048:(c + 1) * 2048]
            kTh = kT[:, c * 2048:(c + 1) * 2048]
            for half in range(2):
                qi0 = 1024 * half
                for kj in range(NT):
                    par = kj % 2
                    sim = bk(2 * par, 2)
                    S.wait("tensor", cA, sim_free[par])
                    for piece in range(2):
                        S.op("tensor", "matmul",
                             sim[:, piece * 512:(piece + 1) * 512],
                             kTh[poff:poff + 64, kj * 128:(kj + 1) * 128],
                             qTh[poff:poff + 64,
                                 qi0 + piece * 512:qi0 + (piece + 1) * 512],
                             start=True, stop=True,
                             inc=(cT if piece == 1 else None))
                    sim_done = cT.n
                    S.wait("scalar", cT, sim_done)
                    S.wait("scalar", cT, p_free[par])
                    S.op("scalar", "activation", p_sb(par), sim, AF.Exp,
                         inc=cA)
                    sim_free[par] = cA.n
                    S.wait("tensor", cA, cA.n)
                    if kj == 0:
                        S.wait("tensor", cV, po_free)
                    for piece in range(2):
                        S.op("tensor", "matmul",
                             bk(4, 2)[0:65, piece * 512:(piece + 1) * 512],
                             Vb[:, kj * 1040 + h * 65:kj * 1040 + (h + 1) * 65],
                             p_sb(par)[:, piece * 512:(piece + 1) * 512],
                             start=(kj == 0), stop=(kj == NT - 1),
                             inc=(cT if piece == 1 else None))
                    p_free[par] = cT.n
                S.wait("vector", cT, p_free[(NT - 1) % 2])
                S.op("vector", "tensor_copy", epi65, bk(4, 2)[0:65, 0:1024],
                     inc=cV)
                po_free = cV.n
                S.wait("vector", cV)
                S.op("vector", "reciprocal", epi65[64:65, :], epi65[64:65, :],
                     inc=cV)
                rec_done = cV.n
                S.wait("tensor", cV, rec_done)
                for piece in range(2):
                    S.op("tensor", "matmul",
                         bk(6, 2)[0:64, piece * 512:(piece + 1) * 512],
                         onesf[64:65, 0:64],
                         epi65[64:65, piece * 512:(piece + 1) * 512],
                         start=True, stop=True,
                         inc=(cT if piece == 1 else None))
                S.wait("vector", cT, cT.n)
                S.op("vector", "tensor_tensor",
                     ao_v(c)[poff:poff + 64, qi0:qi0 + 1024],
                     epi65[0:64, :], bk(6, 2)[0:64, 0:1024], ALU.mult,
                     inc=cV)

        if st == "ao":
            done = cV.n
            for t in range(8):
                S.wait("sync", cV, done)
                S.op("sync", "dma_start", out=out_h[s, t * 128:(t + 1) * 128, :],
                     in_=AR4.ap()[:, t * 1024:(t + 1) * 1024], inc=cS, k=16)
            return True

        S.wait("tensor", cW, wout_ld)
        S.wait("tensor", cA, cA.n)
        S.wait("sync", cT, cT.n)
        S.wait("sync", cV, cV.n)
        y_free = {0: cV.n, 1: cV.n}
        for j in range(NT):
            par = j % 2
            S.wait("sync", cV, y_free[par])
            S.op("sync", "dma_start", out=xin(par),
                 in_=x_h[s, j * 128:(j + 1) * 128, :], inc=cL, k=16)
            ld = cL.n
            S.wait("tensor", cV, y_free[par])
            Y = bk(2 * par, 2)
            for half in range(2):
                for c in range(NCK):
                    S.op("tensor", "matmul", Y[:, half * 512:(half + 1) * 512],
                         ao_v(c)[:, j * 128:(j + 1) * 128],
                         wOUT(c)[:, half * 512:(half + 1) * 512],
                         start=(c == 0), stop=False)
                S.op("tensor", "matmul", Y[:, half * 512:(half + 1) * 512],
                     onesb[0:1, 0:128], bhi[:, half * 512:(half + 1) * 512],
                     start=False, stop=True,
                     inc=(cT if half == 1 else None))
            S.wait("vector", cT, cT.n)
            S.wait("vector", cL, ld)
            S.op("vector", "tensor_add", xw_tile(j), xin(par), Y, inc=cV)
            y_free[par] = cV.n
        if st == "attn":
            stv = cV.n
            for j in range(NT):
                S.wait("sync", cV, stv)
                S.op("sync", "dma_start", out=out_h[s, j * 128:(j + 1) * 128, :],
                     in_=xw_tile(j), inc=cS, k=16)
            return True

        S.wait("gpsimd", cT, cT.n)
        cn2_ld = load_w(cw2_h, 8192, 8, 512)
        gc1_ld = load_w(gw1_h, 12288, 8, 512)
        S.op("gpsimd", "dma_start",
             out=A4[:, 13312:14336].rearrange("p (c n) -> p c n", c=4),
             in_=cw3_h.rearrange("(c p) n -> p c n", p=128), inc=cW, k=16)
        S.op("gpsimd", "dma_start",
             out=A4[:, 14336:15360].rearrange("p (c n) -> p c n", c=4),
             in_=gw2_h.rearrange("(c p) n -> p c n", p=128), inc=cW, k=16)
        smallw_ld = cW.n
        for j in range(NT):
            par = j % 2
            S.op("vector", "reduce_sum", stm(par), xw_tile(j), AX.X, inc=cV)
            S.wait("vector", cV)
            S.op("vector", "tensor_scalar_mul", stm(par), stm(par), 1.0 / DM,
                 inc=cV)
            S.wait("vector", cV)
            S.op("vector", "tensor_single_scalar", fout_tile(j), xw_tile(j),
                 stm(par), ALU.subtract, inc=cV)
            S.op("vector", "scalar_tensor_tensor", trash, fout_tile(j), 1.0,
                 fout_tile(j), ALU.mult, ALU.mult, accum_out=sts(par), inc=cV)
            S.wait("scalar", cV)
            S.op("scalar", "activation", ststd(par), sts(par), AF.Sqrt,
                 scale=c_invdm, bias=c_eps, inc=cA)
            S.wait("vector", cA)
            S.op("vector", "reciprocal", strd(par), ststd(par), inc=cV)
            S.wait("vector", cV)
            S.op("vector", "tensor_single_scalar", fout_tile(j), fout_tile(j),
                 strd(par), ALU.mult, inc=cV)
        ln2_done = cV.n
        if st == "ln2":
            for j in range(NT):
                S.wait("sync", cV, ln2_done)
                S.op("sync", "dma_start", out=out_h[s, j * 128:(j + 1) * 128, :],
                     in_=fout_tile(j), inc=cS, k=16)
            return True

        S.wait("tensor", cV, ln2_done)
        for c in range(NCK):
            for j in range(NT):
                S.op("tensor", "matmul", bk(6)[:, 16 + c:17 + c],
                     fout_tile(j)[:, c * 128:(c + 1) * 128], onesf[:, 0:1],
                     start=(j == 0), stop=(j == NT - 1),
                     inc=(cT if j == NT - 1 else None))
        S.wait("scalar", cT)
        S.op("scalar", "activation", gcb, bk(6)[:, 16:24], AF.Copy,
             scale=c_invS, inc=cA)
        S.wait("tensor", cA)
        S.wait("tensor", cW, gc1_ld)
        for oc in range(4):
            S.wait("tensor", cA, cA.n)
            for c in range(NCK):
                S.op("tensor", "matmul", bk(6)[:, 24 + oc:25 + oc],
                     wGC1(c)[:, oc * 128:(oc + 1) * 128], gcb[:, c:c + 1],
                     start=(c == 0), stop=(c == NCK - 1),
                     inc=(cT if c == NCK - 1 else None))
            S.wait("scalar", cT)
            S.op("scalar", "activation", h1gb[:, oc:oc + 1],
                 bk(6)[:, 24 + oc:25 + oc], AF.Gelu,
                 bias=gcb1[:, oc:oc + 1], inc=cA)
        S.wait("tensor", cW, smallw_ld)
        for oc in range(2):
            S.wait("tensor", cA, cA.n)
            for c in range(4):
                S.op("tensor", "matmul", bk(6)[:, 28 + oc:29 + oc],
                     gc2v(c)[:, oc * 128:(oc + 1) * 128], h1gb[:, c:c + 1],
                     start=(c == 0), stop=(c == 3),
                     inc=(cT if c == 3 else None))
            S.wait("scalar", cT)
            S.op("scalar", "activation", h2gb[:, oc:oc + 1],
                 bk(6)[:, 28 + oc:29 + oc], AF.Gelu,
                 bias=gcb2[:, oc:oc + 1], inc=cA)
        S.wait("tensor", cA, cA.n)
        S.wait("tensor", cW, smallw_setup)
        for c in range(2):
            S.op("tensor", "matmul", bk(6)[0:1, 30:31], gc3b[:, c:c + 1],
                 h2gb[:, c:c + 1], start=(c == 0), stop=(c == 1),
                 inc=(cT if c == 1 else None))
        S.wait("scalar", cT)
        S.op("scalar", "activation", gf_s, bk(6)[0:1, 30:31], AF.Sigmoid,
             bias=gcb3, inc=cA)
        S.wait("tensor", cA)
        S.op("tensor", "matmul", bk(6)[:, 31:32], onesf[0:1, 0:128], gf_s,
             start=True, stop=True, inc=cT)
        S.wait("vector", cT)
        S.op("vector", "tensor_copy", gf128, bk(6)[:, 31:32], inc=cV)
        S.wait("vector", cV)
        S.op("vector", "tensor_scalar_mul", u128, gf128, 1.4, inc=cV)
        S.wait("vector", cV)
        S.op("vector", "tensor_scalar_add", u128, u128, -1.0, inc=cV)
        S.wait("vector", cV)
        S.op("vector", "tensor_mul", B128, u128, ar128, inc=cV)
        S.wait("vector", cV)
        S.op("vector", "tensor_scalar_add", B128, B128, 0.1, inc=cV)
        gf_ready = cV.n

        S.wait("tensor", cW, cn1_ld)
        S.wait("tensor", cW, cn2_ld)
        tp_free = {0: cV.n, 1: cV.n}
        h1_free = {0: cA.n, 1: cA.n}
        h2_free = {0: cA.n, 1: cA.n}
        for cyc in range(3):
            for g in range(4):
                tg = list(range(4 * g, 4 * g + 4))
                for c in range(NCK):
                    par = c % 2
                    tp = bk(2 + par)
                    S.wait("tensor", cV, tp_free[par])
                    for t_i, j in enumerate(tg):
                        S.op("tensor", "transpose",
                             tp[:, t_i * 128:(t_i + 1) * 128],
                             fout_tile(j)[:, c * 128:(c + 1) * 128], identf,
                             inc=(cT if t_i == 3 else None))
                    S.wait("vector", cT, cT.n)
                    S.op("vector", "tensor_copy", outTg(c), tp, inc=cV)
                    tp_free[par] = cV.n
                tgp_done = cV.n
                S.wait("tensor", cV, tgp_done)
                for o in range(NCK):
                    par = o % 2
                    H1 = bk(par)
                    S.wait("tensor", cA, h1_free[par])
                    for c in range(NCK):
                        S.op("tensor", "matmul", H1,
                             wA(c)[:, o * 128:(o + 1) * 128], outTg(c),
                             start=(c == 0), stop=(c == NCK - 1),
                             inc=(cT if c == NCK - 1 else None))
                    S.wait("scalar", cT, cT.n)
                    S.op("scalar", "activation", h1b(o), H1, AF.Gelu,
                         bias=cnb1[:, o:o + 1], inc=cA)
                    h1_free[par] = cA.n
                for o in range(4):
                    par = o % 2
                    H2 = bk(4 + par)
                    S.wait("tensor", cA, h1_free[1] if o == 0 else h2_free[par])
                    if o == 0:
                        S.wait("tensor", cA, h1_free[0])
                    for c in range(NCK):
                        S.op("tensor", "matmul", H2,
                             wCN2(c)[:, o * 128:(o + 1) * 128], h1b(c),
                             start=(c == 0), stop=(c == NCK - 1),
                             inc=(cT if c == NCK - 1 else None))
                    S.wait("scalar", cT, cT.n)
                    S.op("scalar", "activation", h2b(o), H2, AF.Gelu,
                         bias=cnb2[:, o:o + 1], inc=cA)
                    h2_free[par] = cA.n
                for o in range(2):
                    par = o % 2
                    H3 = bk(par)
                    S.wait("tensor", cA, cA.n)
                    for c in range(4):
                        S.op("tensor", "matmul", H3,
                             cn3v(c)[:, o * 128:(o + 1) * 128], h2b(c),
                             start=(c == 0), stop=(c == 3),
                             inc=(cT if c == 3 else None))
                    S.wait("scalar", cT, cT.n)
                    S.op("scalar", "activation", h3b(o), H3, AF.Gelu,
                         bias=cnb3[:, o:o + 1], inc=cA)
                    h1_free[par] = cA.n
                S.wait("tensor", cA, cA.n)
                if cyc == 0 and g == 0:
                    S.wait("tensor", cV, gf_ready)
                for t_i, j in enumerate(tg):
                    for c in range(2):
                        S.op("tensor", "matmul", bk(6)[:, j:j + 1],
                             h3b(c)[:, t_i * 128:(t_i + 1) * 128],
                             cn4b[:, c:c + 1], start=(c == 0), stop=(c == 1),
                             inc=(cT if c == 1 else None))
            S.wait("scalar", cT)
            S.op("scalar", "activation", tfs, bk(6)[:, 0:16], AF.Sigmoid,
                 bias=cnb4c, inc=cA)
            S.wait("vector", cA)
            if cyc == 0:
                S.wait("vector", cV, gf_ready)
            for j in range(NT):
                S.op("vector", "scalar_tensor_tensor", coup[:, j:j + 1],
                     tfs[:, j:j + 1], A128, B128, ALU.mult, ALU.add,
                     inc=(cV if j == NT - 1 else None))
            S.wait("vector", cV)
            S.op("vector", "tensor_scalar_mul", ncoup, coup, -1.0, inc=cV)
            S.wait("vector", cV)
            gp_done = {0: cG.n, 1: cG.n}
            for j in range(NT):
                par = j % 2
                ft = fout_tile(j)
                x0 = ft[:, 0:256]; x1 = ft[:, 256:512]
                x2 = ft[:, 512:768]; x3 = ft[:, 768:1024]
                cj = coup[:, j:j + 1]
                nj = ncoup[:, j:j + 1]
                S.wait("vector", cG, gp_done[par])
                S.op("vector", "scalar_tensor_tensor", tw0(par), x2, cj, x0,
                     ALU.mult, ALU.add)
                S.op("vector", "scalar_tensor_tensor", tw1(par), x3, nj, x1,
                     ALU.mult, ALU.add)
                S.op("vector", "scalar_tensor_tensor", x2, x0, nj, x2,
                     ALU.mult, ALU.add)
                S.op("vector", "scalar_tensor_tensor", x3, x1, cj, x3,
                     ALU.mult, ALU.add, inc=cV)
                S.wait("gpsimd", cV, cV.n)
                S.op("gpsimd", "tensor_copy", x0, tw0(par))
                S.op("gpsimd", "tensor_copy", x1, tw1(par), inc=cG)
                gp_done[par] = cG.n
            S.wait("tensor", cG, cG.n)
            S.wait("tensor", cV, cV.n)
            if st == f"cyc{cyc + 1}":
                done_v, done_g = cV.n, cG.n
                for j in range(NT):
                    S.wait("sync", cV, done_v)
                    S.wait("sync", cG, done_g)
                    S.op("sync", "dma_start",
                         out=out_h[s, j * 128:(j + 1) * 128, :],
                         in_=fout_tile(j), inc=cS, k=16)
                return True

        S.wait("vector", cG, cG.n)
        st_done = {0: cS.n, 1: cS.n}
        for j in range(NT):
            par = j % 2
            S.wait("vector", cS, st_done[par])
            S.op("vector", "tensor_add", stg(par), xw_tile(j), fout_tile(j),
                 inc=cV)
            S.wait("sync", cV, cV.n)
            S.op("sync", "dma_start", out=out_h[s, j * 128:(j + 1) * 128, :],
                 in_=stg(par), inc=cS, k=16)
            st_done[par] = cS.n

    for s in range(2):
        if seq_program(s):
            break

    S.wait("sync", cS, cS.n)

    with es, nc.Block() as block:
        @block.sync
        def _(e):
            for fn in S.ev["sync"]:
                fn(e)

        @block.gpsimd
        def _(e):
            for fn in S.ev["gpsimd"]:
                fn(e)

        @block.vector
        def _(e):
            for fn in S.ev["vector"]:
                fn(e)

        @block.scalar
        def _(e):
            for fn in S.ev["scalar"]:
                fn(e)

        @block.tensor
        def _(e):
            for fn in S.ev["tensor"]:
                fn(e)
    return nc


def _prep_in_maps(inputs):
    x = np.ascontiguousarray(np.asarray(inputs["L0_attn_x"], dtype=np.float32))
    f32 = lambda a: np.ascontiguousarray(np.asarray(a, dtype=np.float32))
    shared = {
        "w_q": f32(inputs["L0_attn_w_q"]),
        "w_kv": f32(inputs["L0_attn_w_kv"]),
        "w_out": f32(inputs["L0_attn_w_out"]),
        "b_out": f32(inputs["L0_attn_b_out"]).reshape(1, DM),
        "scale": f32(inputs["L0_attn_scale"]).reshape(1, NH),
        "cn_w1": f32(inputs["L0_ffn_cn_w1"]),
        "cn_w2": f32(inputs["L0_ffn_cn_w2"]),
        "cn_w3": f32(inputs["L0_ffn_cn_w3"]),
        "cn_w4": f32(inputs["L0_ffn_cn_w4"]).reshape(256, 1),
        "gc_w1": f32(inputs["L0_ffn_gc_w1"]),
        "gc_w2": f32(inputs["L0_ffn_gc_w2"]),
        "gc_w3": f32(inputs["L0_ffn_gc_w3"]).reshape(256, 1),
        "cn_b1": f32(inputs["L0_ffn_cn_b1"]).reshape(8, 128, 1),
        "cn_b2": f32(inputs["L0_ffn_cn_b2"]).reshape(4, 128, 1),
        "cn_b3": f32(inputs["L0_ffn_cn_b3"]).reshape(2, 128, 1),
        "gc_b1": f32(inputs["L0_ffn_gc_b1"]).reshape(4, 128, 1),
        "gc_b2": f32(inputs["L0_ffn_gc_b2"]).reshape(2, 128, 1),
        "cn_b4": f32(inputs["L0_ffn_cn_b4"]).reshape(1, 1),
        "gc_b3": f32(inputs["L0_ffn_gc_b3"]).reshape(1, 1),
        "ar": f32(inputs["L0_ffn_adaptive_range"]).reshape(1, 1),
        "onesf": np.ones((128, 128), dtype=np.float32),
        "identf": np.eye(128, dtype=np.float32),
        "identb": np.eye(128, dtype=np.float32).astype(ml_dtypes.bfloat16),
        "onesb": np.ones((128, 128), dtype=ml_dtypes.bfloat16),
    }
    in_maps = []
    for core in range(8):
        m = dict(shared)
        m["x"] = np.ascontiguousarray(x[2 * core:2 * core + 2])
        in_maps.append(m)
    return in_maps


def kernel(**inputs):
    in_maps = _prep_in_maps(inputs)
    nc = build_nc("full")
    res = run_bass_kernel_spmd(nc, in_maps, core_ids=list(range(8)))
    out = np.concatenate([np.asarray(r["out"], dtype=np.float32)
                          for r in res.results], axis=0)
    return out.reshape(16, SEQ, DM)


# version 05
# speedup vs baseline: 1.2575x; 1.2575x over previous
import numpy as np
import ml_dtypes
import concourse.bass as bass
import concourse.mybir as mybir
from concourse.bass_utils import run_bass_kernel_spmd

F32 = mybir.dt.float32
BF16 = mybir.dt.bfloat16
AF = mybir.ActivationFunctionType
ALU = mybir.AluOpType
AX = mybir.AxisListType

SEQ = 2048
DM = 1024
NH = 16
DH = 64
NT = 16
NCK = 8
ENGS = ("sync", "gpsimd", "vector", "scalar", "tensor")


class Chan:
    def __init__(self, nc, es, name):
        self.sem = es.enter_context(nc.semaphore(name))
        self.n = 0


class Sched:

    def __init__(self):
        self.ev = {k: [] for k in ENGS}

    strict = {}

    def op(self, eng, meth, *args, inc=None, k=1, **kw):
        strict_ch = None
        if meth != "dma_start" and eng in self.strict:
            strict_ch = self.strict[eng]
            if inc is None:
                inc = strict_ch
        sem = None
        if inc is not None:
            inc.n += k
            sem = inc.sem
        def g(e, meth=meth, args=args, kw=kw, sem=sem, k=k):
            ins = getattr(e, meth)(*args, **kw)
            if sem is not None:
                ins.then_inc(sem, k)
        self.ev[eng].append(g)
        if strict_ch is not None:
            self.wait(eng, strict_ch)

    def wait(self, eng, ch, n=None):
        tgt = ch.n if n is None else n
        if tgt <= 0:
            return
        self.ev[eng].append(lambda e, s=ch.sem, t=tgt: e.wait_ge(s, t))


def build_nc(stage="full"):
    nc = bass.Bass("TRN2", target_bir_lowering=False,
                   dynamic_dma_scratch_size=16384)

    def din(name, shape, dt=F32):
        return nc.dram_tensor(name, list(shape), dt, kind="ExternalInput").ap()

    x_h = din("x", [2, SEQ, DM])
    wq_h = din("w_q", [DM, DM])
    wkv_h = din("w_kv", [DM, 2 * DM])
    wout_h = din("w_out", [DM, DM])
    bout_h = din("b_out", [1, DM])
    scale_h = din("scale", [1, NH])
    cw1_h = din("cn_w1", [DM, DM])
    cw2_h = din("cn_w2", [DM, 512])
    cw3_h = din("cn_w3", [512, 256])
    cw4_h = din("cn_w4", [256, 1])
    gw1_h = din("gc_w1", [DM, 512])
    gw2_h = din("gc_w2", [512, 256])
    gw3_h = din("gc_w3", [256, 1])
    cb1_h = din("cn_b1", [8, 128, 1])
    cb2_h = din("cn_b2", [4, 128, 1])
    cb3_h = din("cn_b3", [2, 128, 1])
    gb1_h = din("gc_b1", [4, 128, 1])
    gb2_h = din("gc_b2", [2, 128, 1])
    cb4_h = din("cn_b4", [1, 1])
    gb3_h = din("gc_b3", [1, 1])
    ar_h = din("ar", [1, 1])
    onesf_h = din("onesf", [128, 128])
    identf_h = din("identf", [128, 128])
    identb_h = din("identb", [128, 128], BF16)
    onesb_h = din("onesb", [128, 128], BF16)
    out_h = nc.dram_tensor("out", [2, SEQ, DM], F32, kind="ExternalOutput").ap()

    from contextlib import ExitStack
    es = ExitStack()
    sb = lambda n, s, d=F32: es.enter_context(nc.sbuf_tensor(n, s, d))

    AR0 = sb("AR0", [128, 8192])
    AR1 = sb("AR1", [128, 8192])
    AR2 = sb("AR2", [128, 8192])
    AR3 = sb("AR3", [128, 8320])
    AR4 = sb("AR4", [128, 8192])
    W = sb("W", [128, 8192])
    SCRF = sb("SCRF", [128, 2048])
    ln1T = AR0.ap().bitcast(BF16)
    qT = AR1.ap().bitcast(BF16)
    kT = AR2.ap().bitcast(BF16)
    Vb = AR3.ap().bitcast(BF16)
    A4 = AR4.ap().bitcast(BF16)
    Wb = W.ap().bitcast(BF16)
    SCRB = SCRF.ap().bitcast(BF16)

    def xw_tile(j):
        return (AR0.ap() if j < 8 else AR1.ap())[:, (j % 8) * 1024:(j % 8 + 1) * 1024]

    def fout_tile(j):
        return (AR2.ap() if j < 8 else AR3.ap())[:, (j % 8) * 1024:(j % 8 + 1) * 1024]

    onesf = sb("c_onesf", [128, 128]).ap()
    identf = sb("c_identf", [128, 128]).ap()
    identb = sb("c_identb", [128, 128], BF16).ap()
    onesb = sb("c_onesb", [128, 128], BF16).ap()
    sc_row = sb("sc_row", [1, NH]).ap()
    eS_row = sb("eS_row", [1, NH]).ap()
    eS128 = sb("eS128", [128, NH]).ap()
    ar128 = sb("ar128", [128, 1]).ap()
    cnb4c = sb("cnb4c", [128, 1]).ap()
    A128 = sb("A128", [128, 1]).ap()
    B128 = sb("B128", [128, 1]).ap()
    gf128 = sb("gf128", [128, 1]).ap()
    u128 = sb("u128", [128, 1]).ap()
    bhi = sb("bhi", [1, DM], BF16).ap()
    cnb1 = sb("cnb1", [128, 8]).ap()
    cnb2 = sb("cnb2", [128, 4]).ap()
    cnb3 = sb("cnb3", [128, 2]).ap()
    gcb1 = sb("gcb1", [128, 4]).ap()
    gcb2 = sb("gcb2", [128, 2]).ap()
    gcb3 = sb("gcb3", [1, 1]).ap()
    ar_sb = sb("ar_sb", [1, 1]).ap()
    cnb4_sb = sb("cnb4_sb", [1, 1]).ap()
    c_eps = sb("c_eps", [128, 1]).ap()
    c_invdm = sb("c_invdm", [128, 1]).ap()
    c_invS = sb("c_invS", [128, 1]).ap()
    stm_t = sb("stm", [128, 2]).ap()
    sts_t = sb("sts", [128, 2]).ap()
    ststd_t = sb("ststd", [128, 2]).ap()
    strd_t = sb("strd", [128, 2]).ap()
    ssq16_t = sb("ssq16", [128, 2 * NH]).ap()
    n16_t = sb("n16", [128, 2 * NH]).ap()
    r16_t = sb("r16", [128, 2 * NH]).ap()
    rq16_t = sb("rq16", [128, 2 * NH]).ap()
    stm = lambda p: stm_t[:, p:p + 1]
    sts = lambda p: sts_t[:, p:p + 1]
    ststd = lambda p: ststd_t[:, p:p + 1]
    strd = lambda p: strd_t[:, p:p + 1]
    ssq16 = lambda p: ssq16_t[:, p * NH:(p + 1) * NH]
    n16 = lambda p: n16_t[:, p * NH:(p + 1) * NH]
    r16 = lambda p: r16_t[:, p * NH:(p + 1) * NH]
    rq16 = lambda p: rq16_t[:, p * NH:(p + 1) * NH]
    cn4b = sb("cn4b", [128, 2], BF16).ap()
    gc3b = sb("gc3b", [128, 2], BF16).ap()
    gcb = sb("gcb_t", [128, 8], BF16).ap()
    h1gb = sb("h1gb", [128, 4], BF16).ap()
    h2gb = sb("h2gb", [128, 2], BF16).ap()
    gf_s = sb("gf_s", [1, 1]).ap()
    tfs = sb("tfs", [128, NT]).ap()
    coup = sb("coup", [128, NT]).ap()
    ncoup = sb("ncoup", [128, NT]).ap()

    PSt = es.enter_context(nc.psum_tensor("PS", [128, 4096], F32))
    PS = PSt.ap()
    PSb = PS.bitcast(BF16)
    def bk(i, n=1):
        return PS[:, 512 * i:512 * (i + n)]
    def bkb(i):
        return PSb[:, 1024 * i:1024 * (i + 1)]

    lnt = lambda par: A4[:, par * 1024:(par + 1) * 1024]
    ao_v = lambda c: A4[:, c * 2048:(c + 1) * 2048]
    outTg = lambda c: A4[:, 2048 + c * 512:2048 + (c + 1) * 512]
    h1b = lambda o: A4[:, 6144 + o * 512:6144 + (o + 1) * 512]
    h2b = lambda o: A4[:, 10240 + o * 512:10240 + (o + 1) * 512]
    h3b = lambda o: A4[:, 12288 + o * 512:12288 + (o + 1) * 512]
    cn3v = lambda c: A4[:, 13312 + c * 256:13312 + (c + 1) * 256]
    gc2v = lambda c: A4[:, 14336 + c * 256:14336 + (c + 1) * 256]
    trash = A4[:, 6144:7168]
    xin = lambda par: SCRF.ap()[:, par * 1024:(par + 1) * 1024]
    qsq = lambda p: SCRF.ap()[:, p * 1024:(p + 1) * 1024]
    p_sb = lambda par: SCRB[:, par * 1024:(par + 1) * 1024]
    epi65 = SCRF.ap()[0:65, 1024:2048]
    tw0 = lambda par: SCRF.ap()[:, par * 512:par * 512 + 256]
    tw1 = lambda par: SCRF.ap()[:, par * 512 + 256:par * 512 + 512]
    stg = lambda par: SCRF.ap()[:, par * 1024:(par + 1) * 1024]
    vones = Vb.rearrange("p (t h e) -> p t h e", t=NT, h=NH)[:, :, :, 64:65]

    S = Sched()
    cL = Chan(nc, es, "cL")
    cW = Chan(nc, es, "cW")
    cS = Chan(nc, es, "cS")
    cT = Chan(nc, es, "cT")
    cA = Chan(nc, es, "cA")
    cV = Chan(nc, es, "cV")
    cG = Chan(nc, es, "cG")
    S.strict = {"vector": cV}

    def load_w(h_ap, bf16_col_off, nchunk, ncols):
        out = Wb[:, bf16_col_off:bf16_col_off + nchunk * ncols]
        S.op("gpsimd", "dma_start",
             out=out.rearrange("p (c n) -> p c n", c=nchunk),
             in_=h_ap.rearrange("(c p) n -> p c n", p=128), inc=cW, k=16)
        return cW.n

    wA = lambda c: Wb[:, c * 1024:(c + 1) * 1024]
    wOUT = lambda c: Wb[:, 8192 + c * 1024:8192 + (c + 1) * 1024]
    wCN2 = lambda c: Wb[:, 8192 + c * 512:8192 + (c + 1) * 512]
    wGC1 = lambda c: Wb[:, 12288 + c * 512:12288 + (c + 1) * 512]

    for dst, src in [(onesf, onesf_h), (identf, identf_h), (identb, identb_h),
                     (onesb, onesb_h), (sc_row, scale_h),
                     (ar_sb, ar_h), (cnb4_sb, cb4_h), (gcb3, gb3_h)]:
        S.op("sync", "dma_start", out=dst, in_=src, inc=cL, k=16)
    for dst, src, kk in [(cnb1, cb1_h, 8), (cnb2, cb2_h, 4), (cnb3, cb3_h, 2),
                         (gcb1, gb1_h, 4), (gcb2, gb2_h, 2)]:
        for c in range(kk):
            S.op("sync", "dma_start", out=dst[:, c:c + 1], in_=src[c],
                 inc=cL, k=16)
    S.op("gpsimd", "dma_start", out=bhi, in_=bout_h, inc=cW, k=16)
    S.op("gpsimd", "dma_start", out=cn4b.unsqueeze(2),
         in_=cw4_h.rearrange("(c p) n -> p c n", p=128), inc=cW, k=16,
         allow_slow_non_contiguous=True)
    S.op("gpsimd", "dma_start", out=gc3b.unsqueeze(2),
         in_=gw3_h.rearrange("(c p) n -> p c n", p=128), inc=cW, k=16,
         allow_slow_non_contiguous=True)
    smallw_setup = cW.n

    S.wait("scalar", cL)
    S.op("scalar", "activation", eS_row, sc_row, AF.Exp, inc=cA)
    S.wait("tensor", cA)
    S.wait("tensor", cL)
    S.op("tensor", "matmul", bk(6)[:, 0:16], onesf[0:1, 0:128], eS_row,
         start=True, stop=True)
    S.op("tensor", "matmul", bk(6)[:, 16:17], onesf[0:1, 0:128], ar_sb,
         start=True, stop=True)
    S.op("tensor", "matmul", bk(6)[:, 17:18], onesf[0:1, 0:128], cnb4_sb,
         start=True, stop=True, inc=cT)
    S.wait("vector", cT)
    S.op("vector", "memset", c_eps, 1e-5)
    S.op("vector", "memset", c_invdm, 1.0 / DM)
    S.op("vector", "memset", c_invS, 1.0 / SEQ)
    S.op("vector", "tensor_copy", eS128, bk(6)[:, 0:16])
    S.op("vector", "tensor_copy", ar128, bk(6)[:, 16:17])
    S.op("vector", "tensor_copy", cnb4c, bk(6)[:, 17:18], inc=cV)
    S.wait("vector", cV)
    S.op("vector", "tensor_scalar_mul", A128, ar128, 0.6)
    S.op("vector", "tensor_copy", A128, A128, inc=cV)
    setup_v = cV.n

    tpb_free = {0: 0, 1: 0}

    def transpose_tile_to(dst_arena, j, src_tile, par):
        tp = bkb(6 + par)
        S.wait("tensor", cV, cV.n)
        for c in range(NCK):
            S.op("tensor", "transpose", tp[:, c * 128:(c + 1) * 128],
                 src_tile[:, c * 128:(c + 1) * 128], identb,
                 inc=(cT if c == NCK - 1 else None))
        S.wait("vector", cT)
        dstv = dst_arena.rearrange("p (c n) -> p c n", c=NCK)[
            :, :, j * 128:(j + 1) * 128]
        S.op("vector", "tensor_copy", dstv,
             tp.rearrange("p (c n) -> p c n", c=NCK), inc=cV)
        tpb_free[par] = cV.n

    tpb_free[0] = tpb_free[1] = None

    base_stage, _, _sfx = stage.partition("@")
    dump_s = int(_sfx) if _sfx else 0

    def seq_program(s):
        st = base_stage if s == dump_s else "full"
        if tpb_free[0] is None:
            tpb_free[0] = tpb_free[1] = setup_v
        S.wait("sync", cS, cS.n)
        tile_done = {}
        for j in range(NT):
            par = j % 2
            if j >= 2:
                S.wait("sync", cV, tile_done[j - 2])
            S.op("sync", "dma_start", out=xin(par),
                 in_=x_h[s, j * 128:(j + 1) * 128, :], inc=cL, k=16)
            ld = cL.n
            S.wait("vector", cL, ld)
            S.op("vector", "reduce_sum", stm(par), xin(par), AX.X, inc=cV)
            S.wait("vector", cV)
            S.op("vector", "tensor_scalar_mul", stm(par), stm(par), 1.0 / DM,
                 inc=cV)
            S.wait("vector", cV)
            S.op("vector", "tensor_single_scalar", xin(par), xin(par), stm(par),
                 ALU.subtract, inc=cV)
            S.op("vector", "scalar_tensor_tensor", trash, xin(par), 1.0,
                 xin(par), ALU.mult, ALU.mult, accum_out=sts(par), inc=cV)
            S.wait("scalar", cV)
            S.op("scalar", "activation", ststd(par), sts(par), AF.Sqrt,
                 scale=c_invdm, bias=c_eps, inc=cA)
            S.wait("vector", cA)
            S.op("vector", "reciprocal", strd(par), ststd(par), inc=cV)
            S.wait("vector", cV)
            S.op("vector", "tensor_single_scalar", lnt(par), xin(par), strd(par),
                 ALU.mult, inc=cV)
            tile_done[j] = cV.n
            transpose_tile_to(ln1T, j, lnt(par), par)
        if st == "ln1T":
            done = cV.n
            for t in range(8):
                S.wait("sync", cV, done)
                S.op("sync", "dma_start", out=out_h[s, t * 128:(t + 1) * 128, :],
                     in_=AR0.ap()[:, t * 1024:(t + 1) * 1024], inc=cS, k=16)
            return True

        S.wait("gpsimd", cT, cT.n)
        wk_ld = load_w(wkv_h[:, 0:DM], 0, 8, 1024)
        wv_ld = load_w(wkv_h[:, DM:2 * DM], 8192, 8, 1024)
        S.wait("gpsimd", cV, cV.n)
        S.op("gpsimd", "memset", vones, 1.0, inc=cG)

        def proj_tile(j, wview, par):
            pj = bk(2 * par, 2)
            for half in range(2):
                for c in range(NCK):
                    S.op("tensor", "matmul",
                         pj[:, half * 512:(half + 1) * 512],
                         ln1T[:, c * 2048 + j * 128:c * 2048 + (j + 1) * 128],
                         wview(c)[:, half * 512:(half + 1) * 512],
                         start=(c == 0), stop=(c == NCK - 1),
                         inc=(cT if (c == NCK - 1) else None))
            return pj

        def l2norm_evac(pj, par, with_scale, qsq_guard):
            mm_done = cT.n
            S.wait("scalar", cT, mm_done)
            S.wait("scalar", cV, qsq_guard)
            S.op("scalar", "activation", qsq(par), pj, AF.Square, inc=cA)
            S.wait("vector", cA)
            S.op("vector", "tensor_reduce", ssq16(par),
                 qsq(par).rearrange("p (h e) -> p h e", e=DH), AX.X, ALU.add,
                 inc=cV)
            S.wait("vector", cV)
            S.op("vector", "tensor_scalar_max", ssq16(par), ssq16(par), 1e-24,
                 inc=cV)
            S.wait("scalar", cV)
            S.op("scalar", "activation", n16(par), ssq16(par), AF.Sqrt, inc=cA)
            S.wait("vector", cA)
            S.op("vector", "reciprocal", r16(par), n16(par), inc=cV)
            S.wait("vector", cV)
            mul = r16(par)
            if with_scale:
                S.op("vector", "tensor_mul", rq16(par), r16(par), eS128,
                     inc=cV)
                S.wait("vector", cV)
                mul = rq16(par)
            S.op("vector", "tensor_tensor",
                 lnt(par).rearrange("p (h e) -> p h e", e=DH),
                 pj.rearrange("p (h e) -> p h e", e=DH),
                 mul.unsqueeze(2).broadcast_to([128, NH, DH]),
                 ALU.mult, inc=cV)

        S.wait("tensor", cA, cA.n)
        S.wait("tensor", cW, wk_ld)
        pj_free = {0: cV.n, 1: cV.n}
        for j in range(NT):
            par = j % 2
            S.wait("tensor", cV, pj_free[par])
            pj = proj_tile(j, wA, par)
            l2norm_evac(pj, par, with_scale=False, qsq_guard=pj_free[par])
            pj_free[par] = cV.n
            transpose_tile_to(kT, j, lnt(par), par)
        if st == "lqk":
            done = cV.n
            for t in range(8):
                S.wait("sync", cV, done)
                S.op("sync", "dma_start", out=out_h[s, t * 128:(t + 1) * 128, :],
                     in_=AR0.ap()[:, t * 1024:(t + 1) * 1024], inc=cS, k=16)
                S.op("sync", "dma_start",
                     out=out_h[s, 1024 + t * 128:1024 + (t + 1) * 128, :],
                     in_=AR2.ap()[:, t * 1024:(t + 1) * 1024], inc=cS, k=16)
            return True
        S.wait("gpsimd", cT, cT.n)
        wq_ld = load_w(wq_h, 0, 8, 1024)
        S.wait("tensor", cW, wv_ld)
        S.wait("tensor", cG)
        for j in range(NT):
            par = j % 2
            S.wait("tensor", cV, pj_free[par])
            pj = proj_tile(j, wOUT, par)
            S.wait("vector", cT, cT.n)
            vt = Vb[:, j * 1040:(j + 1) * 1040].rearrange(
                "p (h e) -> p h e", e=65)[:, :, 0:64]
            S.op("vector", "tensor_copy", vt,
                 pj.rearrange("p (h e) -> p h e", e=DH), inc=cV)
            pj_free[par] = cV.n
        S.wait("tensor", cW, wq_ld)
        for j in range(NT):
            par = j % 2
            S.wait("tensor", cV, pj_free[par])
            pj = proj_tile(j, wA, par)
            l2norm_evac(pj, par, with_scale=True, qsq_guard=pj_free[par])
            pj_free[par] = cV.n
            transpose_tile_to(qT, j, lnt(par), par)

        if st == "lq":
            done = cV.n
            for t in range(8):
                S.wait("sync", cV, done)
                S.op("sync", "dma_start", out=out_h[s, t * 128:(t + 1) * 128, :],
                     in_=AR0.ap()[:, t * 1024:(t + 1) * 1024], inc=cS, k=16)
                S.op("sync", "dma_start",
                     out=out_h[s, 1024 + t * 128:1024 + (t + 1) * 128, :],
                     in_=AR1.ap()[:, t * 1024:(t + 1) * 1024], inc=cS, k=16)
            return True
        if st == "qkT":
            done = cV.n
            for t in range(8):
                S.wait("sync", cV, done)
                S.op("sync", "dma_start", out=out_h[s, t * 128:(t + 1) * 128, :],
                     in_=AR1.ap()[:, t * 1024:(t + 1) * 1024], inc=cS, k=16)
                S.op("sync", "dma_start",
                     out=out_h[s, 1024 + t * 128:1024 + (t + 1) * 128, :],
                     in_=AR2.ap()[:, t * 1024:(t + 1) * 1024], inc=cS, k=16)
            return True

        S.wait("gpsimd", cT, cT.n)
        wout_ld = load_w(wout_h, 8192, 8, 1024)
        cn1_ld = load_w(cw1_h, 0, 8, 1024)
        S.wait("tensor", cV, cV.n)
        S.wait("scalar", cV, cV.n)
        po_free = cV.n
        sim_free = {0: (cA, 0), 1: (cA, 0), 2: (cA, 0)}
        p_free = {0: (cT, 0), 1: (cT, 0), 2: (cT, 0)}
        p3 = lambda i: SCRB[:, i * 1024:(i + 1) * 1024]
        for h in range(NH):
            c = h // 2
            poff = 64 * (h % 2)
            qTh = qT[:, c * 2048:(c + 1) * 2048]
            kTh = kT[:, c * 2048:(c + 1) * 2048]
            for half in range(2):
                qi0 = 1024 * half
                exp_done = {}

                def emit_sim(kj):
                    b = kj % 3
                    ch, n = sim_free[b]
                    S.wait("tensor", ch, n)
                    for piece in range(2):
                        S.op("tensor", "matmul",
                             bk(2 * b, 2)[:, piece * 512:(piece + 1) * 512],
                             kTh[poff:poff + 64, kj * 128:(kj + 1) * 128],
                             qTh[poff:poff + 64,
                                 qi0 + piece * 512:qi0 + (piece + 1) * 512],
                             start=True, stop=True,
                             inc=(cT if piece == 1 else None))
                    sim_done = cT.n
                    S.wait("scalar", cT, sim_done)
                    ch, n = p_free[b]
                    S.wait("scalar", ch, n)
                    S.op("scalar", "activation", p3(b), bk(2 * b, 2), AF.Exp,
                         inc=cA)
                    sim_free[b] = (cA, cA.n)
                    exp_done[kj] = cA.n

                def emit_av(kj):
                    b = kj % 3
                    S.wait("tensor", cA, exp_done[kj])
                    if kj == 0:
                        S.wait("tensor", cV, po_free)
                    for piece in range(2):
                        S.op("tensor", "matmul",
                             bk(6, 2)[0:65, piece * 512:(piece + 1) * 512],
                             Vb[:, kj * 1040 + h * 65:kj * 1040 + (h + 1) * 65],
                             p3(b)[:, piece * 512:(piece + 1) * 512],
                             start=(kj == 0), stop=(kj == NT - 1),
                             inc=(cT if piece == 1 else None))
                    p_free[b] = (cT, cT.n)

                for kj in range(NT):
                    emit_sim(kj)
                    if kj >= 2:
                        emit_av(kj - 2)
                emit_av(NT - 2)
                emit_av(NT - 1)
                S.wait("vector", cT, p_free[(NT - 1) % 3][1])
                S.op("vector", "tensor_copy", epi65, bk(6, 2)[0:65, 0:1024],
                     inc=cV)
                po_free = cV.n
                S.wait("vector", cV)
                S.op("vector", "reciprocal", epi65[64:65, :], epi65[64:65, :],
                     inc=cV)
                rec_done = cV.n
                S.wait("tensor", cV, rec_done)
                for piece in range(2):
                    S.op("tensor", "matmul",
                         bk(0, 2)[0:64, piece * 512:(piece + 1) * 512],
                         onesf[64:65, 0:64],
                         epi65[64:65, piece * 512:(piece + 1) * 512],
                         start=True, stop=True,
                         inc=(cT if piece == 1 else None))
                S.wait("vector", cT, cT.n)
                S.op("vector", "tensor_tensor",
                     ao_v(c)[poff:poff + 64, qi0:qi0 + 1024],
                     epi65[0:64, :], bk(0, 2)[0:64, 0:1024], ALU.mult,
                     inc=cV)
                sim_free[0] = (cV, cV.n)
                p_free[2] = (cV, cV.n)

        if st == "ao":
            done = cV.n
            for t in range(8):
                S.wait("sync", cV, done)
                S.op("sync", "dma_start", out=out_h[s, t * 128:(t + 1) * 128, :],
                     in_=AR4.ap()[:, t * 1024:(t + 1) * 1024], inc=cS, k=16)
            return True

        S.wait("tensor", cW, wout_ld)
        S.wait("tensor", cA, cA.n)
        S.wait("sync", cT, cT.n)
        S.wait("sync", cV, cV.n)
        y_free = {0: cV.n, 1: cV.n}
        for j in range(NT):
            par = j % 2
            S.wait("sync", cV, y_free[par])
            S.op("sync", "dma_start", out=xin(par),
                 in_=x_h[s, j * 128:(j + 1) * 128, :], inc=cL, k=16)
            ld = cL.n
            S.wait("tensor", cV, y_free[par])
            Y = bk(2 * par, 2)
            for half in range(2):
                for c in range(NCK):
                    S.op("tensor", "matmul", Y[:, half * 512:(half + 1) * 512],
                         ao_v(c)[:, j * 128:(j + 1) * 128],
                         wOUT(c)[:, half * 512:(half + 1) * 512],
                         start=(c == 0), stop=False)
                S.op("tensor", "matmul", Y[:, half * 512:(half + 1) * 512],
                     onesb[0:1, 0:128], bhi[:, half * 512:(half + 1) * 512],
                     start=False, stop=True,
                     inc=(cT if half == 1 else None))
            S.wait("vector", cT, cT.n)
            S.wait("vector", cL, ld)
            S.op("vector", "tensor_add", xw_tile(j), xin(par), Y, inc=cV)
            y_free[par] = cV.n
        if st == "attn":
            stv = cV.n
            for j in range(NT):
                S.wait("sync", cV, stv)
                S.op("sync", "dma_start", out=out_h[s, j * 128:(j + 1) * 128, :],
                     in_=xw_tile(j), inc=cS, k=16)
            return True

        S.wait("gpsimd", cT, cT.n)
        cn2_ld = load_w(cw2_h, 8192, 8, 512)
        gc1_ld = load_w(gw1_h, 12288, 8, 512)
        S.op("gpsimd", "dma_start",
             out=A4[:, 13312:14336].rearrange("p (c n) -> p c n", c=4),
             in_=cw3_h.rearrange("(c p) n -> p c n", p=128), inc=cW, k=16)
        S.op("gpsimd", "dma_start",
             out=A4[:, 14336:15360].rearrange("p (c n) -> p c n", c=4),
             in_=gw2_h.rearrange("(c p) n -> p c n", p=128), inc=cW, k=16)
        smallw_ld = cW.n
        for j in range(NT):
            par = j % 2
            S.op("vector", "reduce_sum", stm(par), xw_tile(j), AX.X, inc=cV)
            S.wait("vector", cV)
            S.op("vector", "tensor_scalar_mul", stm(par), stm(par), 1.0 / DM,
                 inc=cV)
            S.wait("vector", cV)
            S.op("vector", "tensor_single_scalar", fout_tile(j), xw_tile(j),
                 stm(par), ALU.subtract, inc=cV)
            S.op("vector", "scalar_tensor_tensor", trash, fout_tile(j), 1.0,
                 fout_tile(j), ALU.mult, ALU.mult, accum_out=sts(par), inc=cV)
            S.wait("scalar", cV)
            S.op("scalar", "activation", ststd(par), sts(par), AF.Sqrt,
                 scale=c_invdm, bias=c_eps, inc=cA)
            S.wait("vector", cA)
            S.op("vector", "reciprocal", strd(par), ststd(par), inc=cV)
            S.wait("vector", cV)
            S.op("vector", "tensor_single_scalar", fout_tile(j), fout_tile(j),
                 strd(par), ALU.mult, inc=cV)
        ln2_done = cV.n
        if st == "ln2":
            for j in range(NT):
                S.wait("sync", cV, ln2_done)
                S.op("sync", "dma_start", out=out_h[s, j * 128:(j + 1) * 128, :],
                     in_=fout_tile(j), inc=cS, k=16)
            return True

        S.wait("tensor", cV, ln2_done)
        for c in range(NCK):
            for j in range(NT):
                S.op("tensor", "matmul", bk(6)[:, 16 + c:17 + c],
                     fout_tile(j)[:, c * 128:(c + 1) * 128], onesf[:, 0:1],
                     start=(j == 0), stop=(j == NT - 1),
                     inc=(cT if j == NT - 1 else None))
        S.wait("scalar", cT)
        S.op("scalar", "activation", gcb, bk(6)[:, 16:24], AF.Copy,
             scale=c_invS, inc=cA)
        S.wait("tensor", cA)
        S.wait("tensor", cW, gc1_ld)
        for oc in range(4):
            S.wait("tensor", cA, cA.n)
            for c in range(NCK):
                S.op("tensor", "matmul", bk(6)[:, 24 + oc:25 + oc],
                     wGC1(c)[:, oc * 128:(oc + 1) * 128], gcb[:, c:c + 1],
                     start=(c == 0), stop=(c == NCK - 1),
                     inc=(cT if c == NCK - 1 else None))
            S.wait("scalar", cT)
            S.op("scalar", "activation", h1gb[:, oc:oc + 1],
                 bk(6)[:, 24 + oc:25 + oc], AF.Gelu,
                 bias=gcb1[:, oc:oc + 1], inc=cA)
        S.wait("tensor", cW, smallw_ld)
        for oc in range(2):
            S.wait("tensor", cA, cA.n)
            for c in range(4):
                S.op("tensor", "matmul", bk(6)[:, 28 + oc:29 + oc],
                     gc2v(c)[:, oc * 128:(oc + 1) * 128], h1gb[:, c:c + 1],
                     start=(c == 0), stop=(c == 3),
                     inc=(cT if c == 3 else None))
            S.wait("scalar", cT)
            S.op("scalar", "activation", h2gb[:, oc:oc + 1],
                 bk(6)[:, 28 + oc:29 + oc], AF.Gelu,
                 bias=gcb2[:, oc:oc + 1], inc=cA)
        S.wait("tensor", cA, cA.n)
        S.wait("tensor", cW, smallw_setup)
        for c in range(2):
            S.op("tensor", "matmul", bk(6)[0:1, 30:31], gc3b[:, c:c + 1],
                 h2gb[:, c:c + 1], start=(c == 0), stop=(c == 1),
                 inc=(cT if c == 1 else None))
        S.wait("scalar", cT)
        S.op("scalar", "activation", gf_s, bk(6)[0:1, 30:31], AF.Sigmoid,
             bias=gcb3, inc=cA)
        S.wait("tensor", cA)
        S.op("tensor", "matmul", bk(6)[:, 31:32], onesf[0:1, 0:128], gf_s,
             start=True, stop=True, inc=cT)
        S.wait("vector", cT)
        S.op("vector", "tensor_copy", gf128, bk(6)[:, 31:32], inc=cV)
        S.wait("vector", cV)
        S.op("vector", "tensor_scalar_mul", u128, gf128, 1.4, inc=cV)
        S.wait("vector", cV)
        S.op("vector", "tensor_scalar_add", u128, u128, -1.0, inc=cV)
        S.wait("vector", cV)
        S.op("vector", "tensor_mul", B128, u128, ar128, inc=cV)
        S.wait("vector", cV)
        S.op("vector", "tensor_scalar_add", B128, B128, 0.1, inc=cV)
        gf_ready = cV.n

        S.wait("tensor", cW, cn1_ld)
        S.wait("tensor", cW, cn2_ld)
        tp_free = {0: cV.n, 1: cV.n}
        h1_free = {0: cA.n, 1: cA.n}
        h2_free = {0: cA.n, 1: cA.n}
        for cyc in range(3):
            for g in range(4):
                tg = list(range(4 * g, 4 * g + 4))
                for c in range(NCK):
                    par = c % 2
                    tp = bk(2 + par)
                    S.wait("tensor", cV, tp_free[par])
                    for t_i, j in enumerate(tg):
                        S.op("tensor", "transpose",
                             tp[:, t_i * 128:(t_i + 1) * 128],
                             fout_tile(j)[:, c * 128:(c + 1) * 128], identf,
                             inc=(cT if t_i == 3 else None))
                    S.wait("vector", cT, cT.n)
                    S.op("vector", "tensor_copy", outTg(c), tp, inc=cV)
                    tp_free[par] = cV.n
                tgp_done = cV.n
                S.wait("tensor", cV, tgp_done)
                for o in range(NCK):
                    par = o % 2
                    H1 = bk(par)
                    S.wait("tensor", cA, h1_free[par])
                    for c in range(NCK):
                        S.op("tensor", "matmul", H1,
                             wA(c)[:, o * 128:(o + 1) * 128], outTg(c),
                             start=(c == 0), stop=(c == NCK - 1),
                             inc=(cT if c == NCK - 1 else None))
                    S.wait("scalar", cT, cT.n)
                    S.op("scalar", "activation", h1b(o), H1, AF.Gelu,
                         bias=cnb1[:, o:o + 1], inc=cA)
                    h1_free[par] = cA.n
                for o in range(4):
                    par = o % 2
                    H2 = bk(4 + par)
                    S.wait("tensor", cA, h1_free[1] if o == 0 else h2_free[par])
                    if o == 0:
                        S.wait("tensor", cA, h1_free[0])
                    for c in range(NCK):
                        S.op("tensor", "matmul", H2,
                             wCN2(c)[:, o * 128:(o + 1) * 128], h1b(c),
                             start=(c == 0), stop=(c == NCK - 1),
                             inc=(cT if c == NCK - 1 else None))
                    S.wait("scalar", cT, cT.n)
                    S.op("scalar", "activation", h2b(o), H2, AF.Gelu,
                         bias=cnb2[:, o:o + 1], inc=cA)
                    h2_free[par] = cA.n
                for o in range(2):
                    par = o % 2
                    H3 = bk(par)
                    S.wait("tensor", cA, cA.n)
                    for c in range(4):
                        S.op("tensor", "matmul", H3,
                             cn3v(c)[:, o * 128:(o + 1) * 128], h2b(c),
                             start=(c == 0), stop=(c == 3),
                             inc=(cT if c == 3 else None))
                    S.wait("scalar", cT, cT.n)
                    S.op("scalar", "activation", h3b(o), H3, AF.Gelu,
                         bias=cnb3[:, o:o + 1], inc=cA)
                    h1_free[par] = cA.n
                S.wait("tensor", cA, cA.n)
                if cyc == 0 and g == 0:
                    S.wait("tensor", cV, gf_ready)
                for t_i, j in enumerate(tg):
                    for c in range(2):
                        S.op("tensor", "matmul", bk(6)[:, j:j + 1],
                             h3b(c)[:, t_i * 128:(t_i + 1) * 128],
                             cn4b[:, c:c + 1], start=(c == 0), stop=(c == 1),
                             inc=(cT if c == 1 else None))
            S.wait("scalar", cT)
            S.op("scalar", "activation", tfs, bk(6)[:, 0:16], AF.Sigmoid,
                 bias=cnb4c, inc=cA)
            S.wait("vector", cA)
            if cyc == 0:
                S.wait("vector", cV, gf_ready)
            for j in range(NT):
                S.op("vector", "scalar_tensor_tensor", coup[:, j:j + 1],
                     tfs[:, j:j + 1], A128, B128, ALU.mult, ALU.add,
                     inc=(cV if j == NT - 1 else None))
            S.wait("vector", cV)
            S.op("vector", "tensor_scalar_mul", ncoup, coup, -1.0, inc=cV)
            S.wait("vector", cV)
            gp_done = {0: cG.n, 1: cG.n}
            for j in range(NT):
                par = j % 2
                ft = fout_tile(j)
                x0 = ft[:, 0:256]; x1 = ft[:, 256:512]
                x2 = ft[:, 512:768]; x3 = ft[:, 768:1024]
                cj = coup[:, j:j + 1]
                nj = ncoup[:, j:j + 1]
                S.wait("vector", cG, gp_done[par])
                S.op("vector", "scalar_tensor_tensor", tw0(par), x2, cj, x0,
                     ALU.mult, ALU.add)
                S.op("vector", "scalar_tensor_tensor", tw1(par), x3, nj, x1,
                     ALU.mult, ALU.add)
                S.op("vector", "scalar_tensor_tensor", x2, x0, nj, x2,
                     ALU.mult, ALU.add)
                S.op("vector", "scalar_tensor_tensor", x3, x1, cj, x3,
                     ALU.mult, ALU.add, inc=cV)
                S.wait("gpsimd", cV, cV.n)
                S.op("gpsimd", "tensor_copy", x0, tw0(par))
                S.op("gpsimd", "tensor_copy", x1, tw1(par), inc=cG)
                gp_done[par] = cG.n
            S.wait("tensor", cG, cG.n)
            S.wait("tensor", cV, cV.n)
            if st == f"cyc{cyc + 1}":
                done_v, done_g = cV.n, cG.n
                for j in range(NT):
                    S.wait("sync", cV, done_v)
                    S.wait("sync", cG, done_g)
                    S.op("sync", "dma_start",
                         out=out_h[s, j * 128:(j + 1) * 128, :],
                         in_=fout_tile(j), inc=cS, k=16)
                return True

        S.wait("vector", cG, cG.n)
        st_done = {0: cS.n, 1: cS.n}
        for j in range(NT):
            par = j % 2
            S.wait("vector", cS, st_done[par])
            S.op("vector", "tensor_add", stg(par), xw_tile(j), fout_tile(j),
                 inc=cV)
            S.wait("sync", cV, cV.n)
            S.op("sync", "dma_start", out=out_h[s, j * 128:(j + 1) * 128, :],
                 in_=stg(par), inc=cS, k=16)
            st_done[par] = cS.n

    for s in range(2):
        if seq_program(s):
            break

    S.wait("sync", cS, cS.n)

    with es, nc.Block() as block:
        @block.sync
        def _(e):
            for fn in S.ev["sync"]:
                fn(e)

        @block.gpsimd
        def _(e):
            for fn in S.ev["gpsimd"]:
                fn(e)

        @block.vector
        def _(e):
            for fn in S.ev["vector"]:
                fn(e)

        @block.scalar
        def _(e):
            for fn in S.ev["scalar"]:
                fn(e)

        @block.tensor
        def _(e):
            for fn in S.ev["tensor"]:
                fn(e)
    return nc


def _prep_in_maps(inputs):
    x = np.ascontiguousarray(np.asarray(inputs["L0_attn_x"], dtype=np.float32))
    f32 = lambda a: np.ascontiguousarray(np.asarray(a, dtype=np.float32))
    shared = {
        "w_q": f32(inputs["L0_attn_w_q"]),
        "w_kv": f32(inputs["L0_attn_w_kv"]),
        "w_out": f32(inputs["L0_attn_w_out"]),
        "b_out": f32(inputs["L0_attn_b_out"]).reshape(1, DM),
        "scale": f32(inputs["L0_attn_scale"]).reshape(1, NH),
        "cn_w1": f32(inputs["L0_ffn_cn_w1"]),
        "cn_w2": f32(inputs["L0_ffn_cn_w2"]),
        "cn_w3": f32(inputs["L0_ffn_cn_w3"]),
        "cn_w4": f32(inputs["L0_ffn_cn_w4"]).reshape(256, 1),
        "gc_w1": f32(inputs["L0_ffn_gc_w1"]),
        "gc_w2": f32(inputs["L0_ffn_gc_w2"]),
        "gc_w3": f32(inputs["L0_ffn_gc_w3"]).reshape(256, 1),
        "cn_b1": f32(inputs["L0_ffn_cn_b1"]).reshape(8, 128, 1),
        "cn_b2": f32(inputs["L0_ffn_cn_b2"]).reshape(4, 128, 1),
        "cn_b3": f32(inputs["L0_ffn_cn_b3"]).reshape(2, 128, 1),
        "gc_b1": f32(inputs["L0_ffn_gc_b1"]).reshape(4, 128, 1),
        "gc_b2": f32(inputs["L0_ffn_gc_b2"]).reshape(2, 128, 1),
        "cn_b4": f32(inputs["L0_ffn_cn_b4"]).reshape(1, 1),
        "gc_b3": f32(inputs["L0_ffn_gc_b3"]).reshape(1, 1),
        "ar": f32(inputs["L0_ffn_adaptive_range"]).reshape(1, 1),
        "onesf": np.ones((128, 128), dtype=np.float32),
        "identf": np.eye(128, dtype=np.float32),
        "identb": np.eye(128, dtype=np.float32).astype(ml_dtypes.bfloat16),
        "onesb": np.ones((128, 128), dtype=ml_dtypes.bfloat16),
    }
    in_maps = []
    for core in range(8):
        m = dict(shared)
        m["x"] = np.ascontiguousarray(x[2 * core:2 * core + 2])
        in_maps.append(m)
    return in_maps


def kernel(**inputs):
    in_maps = _prep_in_maps(inputs)
    nc = build_nc("full")
    res = run_bass_kernel_spmd(nc, in_maps, core_ids=list(range(8)))
    out = np.concatenate([np.asarray(r["out"], dtype=np.float32)
                          for r in res.results], axis=0)
    return out.reshape(16, SEQ, DM)


# version 07
# speedup vs baseline: 1.3172x; 1.0475x over previous
import numpy as np
import ml_dtypes
import concourse.bass as bass
import concourse.mybir as mybir
from concourse.bass_utils import run_bass_kernel_spmd

F32 = mybir.dt.float32
BF16 = mybir.dt.bfloat16
AF = mybir.ActivationFunctionType
ALU = mybir.AluOpType
AX = mybir.AxisListType

SEQ = 2048
DM = 1024
NH = 16
DH = 64
NT = 16
NCK = 8
ENGS = ("sync", "gpsimd", "vector", "scalar", "tensor")


class Chan:
    def __init__(self, nc, es, name):
        self.sem = es.enter_context(nc.semaphore(name))
        self.n = 0


class Sched:

    def __init__(self):
        self.ev = {k: [] for k in ENGS}

    strict = {}

    def op(self, eng, meth, *args, inc=None, k=1, **kw):
        strict_ch = None
        if meth != "dma_start" and eng in self.strict:
            strict_ch = self.strict[eng]
            if inc is None:
                inc = strict_ch
        sem = None
        if inc is not None:
            inc.n += k
            sem = inc.sem
        def g(e, meth=meth, args=args, kw=kw, sem=sem, k=k):
            ins = getattr(e, meth)(*args, **kw)
            if sem is not None:
                ins.then_inc(sem, k)
        self.ev[eng].append(g)
        if strict_ch is not None:
            self.wait(eng, strict_ch)

    def wait(self, eng, ch, n=None):
        tgt = ch.n if n is None else n
        if tgt <= 0:
            return
        self.ev[eng].append(lambda e, s=ch.sem, t=tgt: e.wait_ge(s, t))


def build_nc(stage="full"):
    nc = bass.Bass("TRN2", target_bir_lowering=False,
                   dynamic_dma_scratch_size=16384)

    def din(name, shape, dt=F32):
        return nc.dram_tensor(name, list(shape), dt, kind="ExternalInput").ap()

    x_h = din("x", [2, SEQ, DM])
    wq_h = din("w_q", [DM, DM])
    wkv_h = din("w_kv", [DM, 2 * DM])
    wout_h = din("w_out", [DM, DM])
    bout_h = din("b_out", [1, DM])
    scale_h = din("scale", [1, NH])
    cw1_h = din("cn_w1", [DM, DM])
    cw2_h = din("cn_w2", [DM, 512])
    cw3_h = din("cn_w3", [512, 256])
    cw4_h = din("cn_w4", [256, 1])
    gw1_h = din("gc_w1", [DM, 512])
    gw2_h = din("gc_w2", [512, 256])
    gw3_h = din("gc_w3", [256, 1])
    cb1_h = din("cn_b1", [8, 128, 1])
    cb2_h = din("cn_b2", [4, 128, 1])
    cb3_h = din("cn_b3", [2, 128, 1])
    gb1_h = din("gc_b1", [4, 128, 1])
    gb2_h = din("gc_b2", [2, 128, 1])
    cb4_h = din("cn_b4", [1, 1])
    gb3_h = din("gc_b3", [1, 1])
    ar_h = din("ar", [1, 1])
    onesf_h = din("onesf", [128, 128])
    identf_h = din("identf", [128, 128])
    identb_h = din("identb", [128, 128], BF16)
    onesb_h = din("onesb", [128, 128], BF16)
    out_h = nc.dram_tensor("out", [2, SEQ, DM], F32, kind="ExternalOutput").ap()

    from contextlib import ExitStack
    es = ExitStack()
    sb = lambda n, s, d=F32: es.enter_context(nc.sbuf_tensor(n, s, d))

    AR0 = sb("AR0", [128, 8192])
    AR1 = sb("AR1", [128, 8192])
    AR2 = sb("AR2", [128, 8192])
    AR3 = sb("AR3", [128, 8320])
    AR4 = sb("AR4", [128, 8192])
    W = sb("W", [128, 8192])
    SCRF = sb("SCRF", [128, 2048])
    ln1T = AR0.ap().bitcast(BF16)
    qT = AR1.ap().bitcast(BF16)
    kT = AR2.ap().bitcast(BF16)
    Vb = AR3.ap().bitcast(BF16)
    A4 = AR4.ap().bitcast(BF16)
    Wb = W.ap().bitcast(BF16)
    SCRB = SCRF.ap().bitcast(BF16)

    def xw_tile(j):
        return (AR0.ap() if j < 8 else AR1.ap())[:, (j % 8) * 1024:(j % 8 + 1) * 1024]

    def fout_tile(j):
        return (AR2.ap() if j < 8 else AR3.ap())[:, (j % 8) * 1024:(j % 8 + 1) * 1024]

    onesf = sb("c_onesf", [128, 128]).ap()
    identf = sb("c_identf", [128, 128]).ap()
    identb = sb("c_identb", [128, 128], BF16).ap()
    onesb = sb("c_onesb", [128, 128], BF16).ap()
    sc_row = sb("sc_row", [1, NH]).ap()
    eS_row = sb("eS_row", [1, NH]).ap()
    eS128 = sb("eS128", [128, NH]).ap()
    ar128 = sb("ar128", [128, 1]).ap()
    cnb4c = sb("cnb4c", [128, 1]).ap()
    A128 = sb("A128", [128, 1]).ap()
    B128 = sb("B128", [128, 1]).ap()
    gf128 = sb("gf128", [128, 1]).ap()
    u128 = sb("u128", [128, 1]).ap()
    bhi = sb("bhi", [1, DM], BF16).ap()
    cnb1 = sb("cnb1", [128, 8]).ap()
    cnb2 = sb("cnb2", [128, 4]).ap()
    cnb3 = sb("cnb3", [128, 2]).ap()
    gcb1 = sb("gcb1", [128, 4]).ap()
    gcb2 = sb("gcb2", [128, 2]).ap()
    gcb3 = sb("gcb3", [1, 1]).ap()
    ar_sb = sb("ar_sb", [1, 1]).ap()
    cnb4_sb = sb("cnb4_sb", [1, 1]).ap()
    c_eps = sb("c_eps", [128, 1]).ap()
    c_neg1 = sb("c_neg1", [128, 1]).ap()
    c_invdm = sb("c_invdm", [128, 1]).ap()
    c_invS = sb("c_invS", [128, 1]).ap()
    stm_t = sb("stm", [128, 2]).ap()
    sts_t = sb("sts", [128, 2]).ap()
    ststd_t = sb("ststd", [128, 2]).ap()
    strd_t = sb("strd", [128, 2]).ap()
    ssq16_t = sb("ssq16", [128, 2 * NH]).ap()
    n16_t = sb("n16", [128, 2 * NH]).ap()
    r16_t = sb("r16", [128, 2 * NH]).ap()
    rq16_t = sb("rq16", [128, 2 * NH]).ap()
    stm = lambda p: stm_t[:, p:p + 1]
    sts = lambda p: sts_t[:, p:p + 1]
    ststd = lambda p: ststd_t[:, p:p + 1]
    strd = lambda p: strd_t[:, p:p + 1]
    ssq16 = lambda p: ssq16_t[:, p * NH:(p + 1) * NH]
    n16 = lambda p: n16_t[:, p * NH:(p + 1) * NH]
    r16 = lambda p: r16_t[:, p * NH:(p + 1) * NH]
    rq16 = lambda p: rq16_t[:, p * NH:(p + 1) * NH]
    cn4b = sb("cn4b", [128, 2], BF16).ap()
    gc3b = sb("gc3b", [128, 2], BF16).ap()
    gcb = sb("gcb_t", [128, 8], BF16).ap()
    h1gb = sb("h1gb", [128, 4], BF16).ap()
    h2gb = sb("h2gb", [128, 2], BF16).ap()
    gf_s = sb("gf_s", [1, 1]).ap()
    tfs = sb("tfs", [128, NT]).ap()
    coup = sb("coup", [128, NT]).ap()
    ncoup = sb("ncoup", [128, NT]).ap()

    PSt = es.enter_context(nc.psum_tensor("PS", [128, 4096], F32))
    PS = PSt.ap()
    PSb = PS.bitcast(BF16)
    def bk(i, n=1):
        return PS[:, 512 * i:512 * (i + n)]
    def bkb(i):
        return PSb[:, 1024 * i:1024 * (i + 1)]

    lnt = lambda par: A4[:, par * 1024:(par + 1) * 1024]
    ao_v = lambda c: A4[:, c * 2048:(c + 1) * 2048]
    outTg = lambda c: A4[:, 2048 + c * 512:2048 + (c + 1) * 512]
    h1b = lambda o: A4[:, 6144 + o * 512:6144 + (o + 1) * 512]
    h2b = lambda o: A4[:, 10240 + o * 512:10240 + (o + 1) * 512]
    h3b = lambda o: A4[:, 12288 + o * 512:12288 + (o + 1) * 512]
    cn3v = lambda c: A4[:, 13312 + c * 256:13312 + (c + 1) * 256]
    gc2v = lambda c: A4[:, 14336 + c * 256:14336 + (c + 1) * 256]
    trash = A4[:, 6144:7168]
    xin = lambda par: SCRF.ap()[:, par * 1024:(par + 1) * 1024]
    qsq = lambda p: SCRF.ap()[:, p * 1024:(p + 1) * 1024]
    p_sb = lambda par: SCRB[:, par * 1024:(par + 1) * 1024]
    epi65 = SCRF.ap()[0:65, 1024:2048]
    tw0 = lambda par: SCRF.ap()[:, par * 512:par * 512 + 256]
    tw1 = lambda par: SCRF.ap()[:, par * 512 + 256:par * 512 + 512]
    stg = lambda par: SCRF.ap()[:, par * 1024:(par + 1) * 1024]
    vones = Vb.rearrange("p (t h e) -> p t h e", t=NT, h=NH)[:, :, :, 64:65]

    S = Sched()
    cL = Chan(nc, es, "cL")
    cW = Chan(nc, es, "cW")
    cS = Chan(nc, es, "cS")
    cT = Chan(nc, es, "cT")
    cA = Chan(nc, es, "cA")
    cV = Chan(nc, es, "cV")
    cG = Chan(nc, es, "cG")
    S.strict = {"vector": cV}

    def load_w(h_ap, bf16_col_off, nchunk, ncols):
        out = Wb[:, bf16_col_off:bf16_col_off + nchunk * ncols]
        S.op("gpsimd", "dma_start",
             out=out.rearrange("p (c n) -> p c n", c=nchunk),
             in_=h_ap.rearrange("(c p) n -> p c n", p=128), inc=cW, k=16)
        return cW.n

    wA = lambda c: Wb[:, c * 1024:(c + 1) * 1024]
    wOUT = lambda c: Wb[:, 8192 + c * 1024:8192 + (c + 1) * 1024]
    wCN2 = lambda c: Wb[:, 8192 + c * 512:8192 + (c + 1) * 512]
    wGC1 = lambda c: Wb[:, 12288 + c * 512:12288 + (c + 1) * 512]

    for dst, src in [(onesf, onesf_h), (identf, identf_h), (identb, identb_h),
                     (onesb, onesb_h), (sc_row, scale_h),
                     (ar_sb, ar_h), (cnb4_sb, cb4_h), (gcb3, gb3_h)]:
        S.op("sync", "dma_start", out=dst, in_=src, inc=cL, k=16)
    for dst, src, kk in [(cnb1, cb1_h, 8), (cnb2, cb2_h, 4), (cnb3, cb3_h, 2),
                         (gcb1, gb1_h, 4), (gcb2, gb2_h, 2)]:
        for c in range(kk):
            S.op("sync", "dma_start", out=dst[:, c:c + 1], in_=src[c],
                 inc=cL, k=16)
    S.op("gpsimd", "dma_start", out=bhi, in_=bout_h, inc=cW, k=16)
    S.op("gpsimd", "dma_start", out=cn4b.unsqueeze(2),
         in_=cw4_h.rearrange("(c p) n -> p c n", p=128), inc=cW, k=16,
         allow_slow_non_contiguous=True)
    S.op("gpsimd", "dma_start", out=gc3b.unsqueeze(2),
         in_=gw3_h.rearrange("(c p) n -> p c n", p=128), inc=cW, k=16,
         allow_slow_non_contiguous=True)
    smallw_setup = cW.n

    S.wait("scalar", cL)
    S.op("scalar", "activation", eS_row, sc_row, AF.Exp, inc=cA)
    S.wait("tensor", cA)
    S.wait("tensor", cL)
    S.op("tensor", "matmul", bk(6)[:, 0:16], onesf[0:1, 0:128], eS_row,
         start=True, stop=True)
    S.op("tensor", "matmul", bk(6)[:, 16:17], onesf[0:1, 0:128], ar_sb,
         start=True, stop=True)
    S.op("tensor", "matmul", bk(6)[:, 17:18], onesf[0:1, 0:128], cnb4_sb,
         start=True, stop=True, inc=cT)
    S.wait("vector", cT)
    S.op("vector", "memset", c_eps, 1e-5)
    S.op("vector", "memset", c_neg1, -1.0)
    S.op("vector", "memset", c_invdm, 1.0 / DM)
    S.op("vector", "memset", c_invS, 1.0 / SEQ)
    S.op("vector", "tensor_copy", eS128, bk(6)[:, 0:16])
    S.op("vector", "tensor_copy", ar128, bk(6)[:, 16:17])
    S.op("vector", "tensor_copy", cnb4c, bk(6)[:, 17:18], inc=cV)
    S.wait("vector", cV)
    S.op("vector", "tensor_scalar_mul", A128, ar128, 0.6)
    S.op("vector", "tensor_copy", A128, A128, inc=cV)
    setup_v = cV.n

    tpb_free = {0: 0, 1: 0}

    def transpose_tile_to(dst_arena, j, src_tile, par):
        tp = bkb(6 + par)
        S.wait("tensor", cV, cV.n)
        for c in range(NCK):
            S.op("tensor", "transpose", tp[:, c * 128:(c + 1) * 128],
                 src_tile[:, c * 128:(c + 1) * 128], identb,
                 inc=(cT if c == NCK - 1 else None))
        S.wait("vector", cT)
        dstv = dst_arena.rearrange("p (c n) -> p c n", c=NCK)[
            :, :, j * 128:(j + 1) * 128]
        S.op("vector", "tensor_copy", dstv,
             tp.rearrange("p (c n) -> p c n", c=NCK), inc=cV)
        tpb_free[par] = cV.n

    tpb_free[0] = tpb_free[1] = None

    base_stage, _, _sfx = stage.partition("@")
    dump_s = int(_sfx) if _sfx else 0

    def seq_program(s):
        st = base_stage if s == dump_s else "full"
        if tpb_free[0] is None:
            tpb_free[0] = tpb_free[1] = setup_v
        S.wait("sync", cS, cS.n)
        tile_done = {}
        for j in range(NT):
            par = j % 2
            if j >= 2:
                S.wait("sync", cV, tile_done[j - 2])
            S.op("sync", "dma_start", out=xin(par),
                 in_=x_h[s, j * 128:(j + 1) * 128, :], inc=cL, k=16)
            ld = cL.n
            S.wait("vector", cL, ld)
            S.op("vector", "reduce_sum", stm(par), xin(par), AX.X, inc=cV)
            S.wait("vector", cV)
            S.op("vector", "tensor_scalar_mul", stm(par), stm(par), 1.0 / DM,
                 inc=cV)
            S.wait("vector", cV)
            S.op("vector", "tensor_single_scalar", xin(par), xin(par), stm(par),
                 ALU.subtract, inc=cV)
            S.op("vector", "scalar_tensor_tensor", trash, xin(par), 1.0,
                 xin(par), ALU.mult, ALU.mult, accum_out=sts(par), inc=cV)
            S.wait("scalar", cV)
            S.op("scalar", "activation", ststd(par), sts(par), AF.Sqrt,
                 scale=c_invdm, bias=c_eps, inc=cA)
            S.wait("vector", cA)
            S.op("vector", "reciprocal", strd(par), ststd(par), inc=cV)
            S.wait("vector", cV)
            S.op("vector", "tensor_single_scalar", lnt(par), xin(par), strd(par),
                 ALU.mult, inc=cV)
            tile_done[j] = cV.n
            transpose_tile_to(ln1T, j, lnt(par), par)
        if st == "ln1T":
            done = cV.n
            for t in range(8):
                S.wait("sync", cV, done)
                S.op("sync", "dma_start", out=out_h[s, t * 128:(t + 1) * 128, :],
                     in_=AR0.ap()[:, t * 1024:(t + 1) * 1024], inc=cS, k=16)
            return True

        S.wait("gpsimd", cT, cT.n)
        wk_ld = load_w(wkv_h[:, 0:DM], 0, 8, 1024)
        wv_ld = load_w(wkv_h[:, DM:2 * DM], 8192, 8, 1024)
        S.wait("gpsimd", cV, cV.n)
        S.op("gpsimd", "memset", vones, 1.0, inc=cG)

        def proj_tile(j, wview, par):
            pj = bk(2 * par, 2)
            for half in range(2):
                for c in range(NCK):
                    S.op("tensor", "matmul",
                         pj[:, half * 512:(half + 1) * 512],
                         ln1T[:, c * 2048 + j * 128:c * 2048 + (j + 1) * 128],
                         wview(c)[:, half * 512:(half + 1) * 512],
                         start=(c == 0), stop=(c == NCK - 1),
                         inc=(cT if (c == NCK - 1) else None))
            return pj

        def l2norm_evac(pj, par, with_scale, qsq_guard):
            mm_done = cT.n
            S.wait("scalar", cT, mm_done)
            S.wait("scalar", cV, qsq_guard)
            S.op("scalar", "activation", qsq(par), pj, AF.Square, inc=cA)
            S.wait("vector", cA)
            S.op("vector", "tensor_reduce", ssq16(par),
                 qsq(par).rearrange("p (h e) -> p h e", e=DH), AX.X, ALU.add,
                 inc=cV)
            S.wait("vector", cV)
            S.op("vector", "tensor_scalar_max", ssq16(par), ssq16(par), 1e-24,
                 inc=cV)
            S.wait("scalar", cV)
            S.op("scalar", "activation", n16(par), ssq16(par), AF.Sqrt, inc=cA)
            S.wait("vector", cA)
            S.op("vector", "reciprocal", r16(par), n16(par), inc=cV)
            S.wait("vector", cV)
            mul = r16(par)
            if with_scale:
                S.op("vector", "tensor_mul", rq16(par), r16(par), eS128,
                     inc=cV)
                S.wait("vector", cV)
                mul = rq16(par)
            S.op("vector", "tensor_tensor",
                 lnt(par).rearrange("p (h e) -> p h e", e=DH),
                 pj.rearrange("p (h e) -> p h e", e=DH),
                 mul.unsqueeze(2).broadcast_to([128, NH, DH]),
                 ALU.mult, inc=cV)

        S.wait("tensor", cA, cA.n)
        S.wait("tensor", cW, wk_ld)
        pj_free = {0: cV.n, 1: cV.n}
        for j in range(NT):
            par = j % 2
            S.wait("tensor", cV, pj_free[par])
            pj = proj_tile(j, wA, par)
            l2norm_evac(pj, par, with_scale=False, qsq_guard=pj_free[par])
            pj_free[par] = cV.n
            transpose_tile_to(kT, j, lnt(par), par)
        if st == "lqk":
            done = cV.n
            for t in range(8):
                S.wait("sync", cV, done)
                S.op("sync", "dma_start", out=out_h[s, t * 128:(t + 1) * 128, :],
                     in_=AR0.ap()[:, t * 1024:(t + 1) * 1024], inc=cS, k=16)
                S.op("sync", "dma_start",
                     out=out_h[s, 1024 + t * 128:1024 + (t + 1) * 128, :],
                     in_=AR2.ap()[:, t * 1024:(t + 1) * 1024], inc=cS, k=16)
            return True
        S.wait("gpsimd", cT, cT.n)
        wq_ld = load_w(wq_h, 0, 8, 1024)
        S.wait("tensor", cW, wv_ld)
        S.wait("tensor", cG)
        for j in range(NT):
            par = j % 2
            S.wait("tensor", cV, pj_free[par])
            pj = proj_tile(j, wOUT, par)
            S.wait("vector", cT, cT.n)
            vt = Vb[:, j * 1040:(j + 1) * 1040].rearrange(
                "p (h e) -> p h e", e=65)[:, :, 0:64]
            S.op("vector", "tensor_copy", vt,
                 pj.rearrange("p (h e) -> p h e", e=DH), inc=cV)
            pj_free[par] = cV.n
        S.wait("tensor", cW, wq_ld)
        for j in range(NT):
            par = j % 2
            S.wait("tensor", cV, pj_free[par])
            pj = proj_tile(j, wA, par)
            l2norm_evac(pj, par, with_scale=True, qsq_guard=pj_free[par])
            pj_free[par] = cV.n
            transpose_tile_to(qT, j, lnt(par), par)

        if st == "lq":
            done = cV.n
            for t in range(8):
                S.wait("sync", cV, done)
                S.op("sync", "dma_start", out=out_h[s, t * 128:(t + 1) * 128, :],
                     in_=AR0.ap()[:, t * 1024:(t + 1) * 1024], inc=cS, k=16)
                S.op("sync", "dma_start",
                     out=out_h[s, 1024 + t * 128:1024 + (t + 1) * 128, :],
                     in_=AR1.ap()[:, t * 1024:(t + 1) * 1024], inc=cS, k=16)
            return True
        if st == "qkT":
            done = cV.n
            for t in range(8):
                S.wait("sync", cV, done)
                S.op("sync", "dma_start", out=out_h[s, t * 128:(t + 1) * 128, :],
                     in_=AR1.ap()[:, t * 1024:(t + 1) * 1024], inc=cS, k=16)
                S.op("sync", "dma_start",
                     out=out_h[s, 1024 + t * 128:1024 + (t + 1) * 128, :],
                     in_=AR2.ap()[:, t * 1024:(t + 1) * 1024], inc=cS, k=16)
            return True

        S.wait("gpsimd", cT, cT.n)
        wout_ld = load_w(wout_h, 8192, 8, 1024)
        cn1_ld = load_w(cw1_h, 0, 8, 1024)
        S.wait("tensor", cV, cV.n)
        S.wait("scalar", cV, cV.n)
        po_free = cV.n
        sim_free = {0: (cA, 0), 1: (cA, 0), 2: (cA, 0)}
        p_free = {0: (cT, 0), 1: (cT, 0), 2: (cT, 0)}
        p3 = lambda i: SCRB[:, i * 1024:(i + 1) * 1024]
        for h in range(NH):
            c = h // 2
            poff = 64 * (h % 2)
            qTh = qT[:, c * 2048:(c + 1) * 2048]
            kTh = kT[:, c * 2048:(c + 1) * 2048]
            for half in range(2):
                qi0 = 1024 * half
                exp_done = {}

                def emit_sim(kj):
                    b = kj % 3
                    ch, n = sim_free[b]
                    S.wait("tensor", ch, n)
                    for piece in range(2):
                        S.op("tensor", "matmul",
                             bk(2 * b, 2)[:, piece * 512:(piece + 1) * 512],
                             kTh[poff:poff + 64, kj * 128:(kj + 1) * 128],
                             qTh[poff:poff + 64,
                                 qi0 + piece * 512:qi0 + (piece + 1) * 512],
                             start=True, stop=True,
                             inc=(cT if piece == 1 else None))
                    sim_done = cT.n
                    S.wait("scalar", cT, sim_done)
                    ch, n = p_free[b]
                    S.wait("scalar", ch, n)
                    S.op("scalar", "activation", p3(b), bk(2 * b, 2), AF.Exp,
                         inc=cA)
                    sim_free[b] = (cA, cA.n)
                    exp_done[kj] = cA.n

                def emit_av(kj):
                    b = kj % 3
                    S.wait("tensor", cA, exp_done[kj])
                    if kj == 0:
                        S.wait("tensor", cV, po_free)
                    for piece in range(2):
                        S.op("tensor", "matmul",
                             bk(6, 2)[0:65, piece * 512:(piece + 1) * 512],
                             Vb[:, kj * 1040 + h * 65:kj * 1040 + (h + 1) * 65],
                             p3(b)[:, piece * 512:(piece + 1) * 512],
                             start=(kj == 0), stop=(kj == NT - 1),
                             inc=(cT if piece == 1 else None))
                    p_free[b] = (cT, cT.n)

                for kj in range(NT):
                    emit_sim(kj)
                    if kj >= 2:
                        emit_av(kj - 2)
                emit_av(NT - 2)
                emit_av(NT - 1)
                S.wait("vector", cT, p_free[(NT - 1) % 3][1])
                S.op("vector", "tensor_copy", epi65, bk(6, 2)[0:65, 0:1024],
                     inc=cV)
                po_free = cV.n
                S.wait("scalar", cV, cV.n)
                S.op("scalar", "activation", epi65[64:65, :], epi65[64:65, :],
                     AF.Ln)
                S.op("scalar", "activation", epi65[64:65, :], epi65[64:65, :],
                     AF.Exp, scale=c_neg1[0:1, :], inc=cA)
                rec_done = cA.n
                S.wait("tensor", cA, rec_done)
                for piece in range(2):
                    S.op("tensor", "matmul",
                         bk(0, 2)[0:64, piece * 512:(piece + 1) * 512],
                         onesf[64:65, 0:64],
                         epi65[64:65, piece * 512:(piece + 1) * 512],
                         start=True, stop=True,
                         inc=(cT if piece == 1 else None))
                S.wait("vector", cT, cT.n)
                S.op("vector", "tensor_tensor",
                     ao_v(c)[poff:poff + 64, qi0:qi0 + 1024],
                     epi65[0:64, :], bk(0, 2)[0:64, 0:1024], ALU.mult,
                     inc=cV)
                sim_free[0] = (cV, cV.n)
                p_free[2] = (cV, cV.n)

        if st == "ao":
            done = cV.n
            for t in range(8):
                S.wait("sync", cV, done)
                S.op("sync", "dma_start", out=out_h[s, t * 128:(t + 1) * 128, :],
                     in_=AR4.ap()[:, t * 1024:(t + 1) * 1024], inc=cS, k=16)
            return True

        S.wait("tensor", cW, wout_ld)
        S.wait("tensor", cA, cA.n)
        S.wait("sync", cT, cT.n)
        S.wait("sync", cV, cV.n)
        y_free = {0: cV.n, 1: cV.n}
        for j in range(NT):
            par = j % 2
            S.wait("sync", cV, y_free[par])
            S.op("sync", "dma_start", out=xin(par),
                 in_=x_h[s, j * 128:(j + 1) * 128, :], inc=cL, k=16)
            ld = cL.n
            S.wait("tensor", cV, y_free[par])
            Y = bk(2 * par, 2)
            for half in range(2):
                for c in range(NCK):
                    S.op("tensor", "matmul", Y[:, half * 512:(half + 1) * 512],
                         ao_v(c)[:, j * 128:(j + 1) * 128],
                         wOUT(c)[:, half * 512:(half + 1) * 512],
                         start=(c == 0), stop=False)
                S.op("tensor", "matmul", Y[:, half * 512:(half + 1) * 512],
                     onesb[0:1, 0:128], bhi[:, half * 512:(half + 1) * 512],
                     start=False, stop=True,
                     inc=(cT if half == 1 else None))
            S.wait("vector", cT, cT.n)
            S.wait("vector", cL, ld)
            S.op("vector", "tensor_add", xw_tile(j), xin(par), Y, inc=cV)
            y_free[par] = cV.n
        if st == "attn":
            stv = cV.n
            for j in range(NT):
                S.wait("sync", cV, stv)
                S.op("sync", "dma_start", out=out_h[s, j * 128:(j + 1) * 128, :],
                     in_=xw_tile(j), inc=cS, k=16)
            return True

        S.wait("gpsimd", cT, cT.n)
        cn2_ld = load_w(cw2_h, 8192, 8, 512)
        gc1_ld = load_w(gw1_h, 12288, 8, 512)
        S.op("gpsimd", "dma_start",
             out=A4[:, 13312:14336].rearrange("p (c n) -> p c n", c=4),
             in_=cw3_h.rearrange("(c p) n -> p c n", p=128), inc=cW, k=16)
        S.op("gpsimd", "dma_start",
             out=A4[:, 14336:15360].rearrange("p (c n) -> p c n", c=4),
             in_=gw2_h.rearrange("(c p) n -> p c n", p=128), inc=cW, k=16)
        smallw_ld = cW.n
        for j in range(NT):
            par = j % 2
            S.op("vector", "reduce_sum", stm(par), xw_tile(j), AX.X, inc=cV)
            S.wait("vector", cV)
            S.op("vector", "tensor_scalar_mul", stm(par), stm(par), 1.0 / DM,
                 inc=cV)
            S.wait("vector", cV)
            S.op("vector", "tensor_single_scalar", fout_tile(j), xw_tile(j),
                 stm(par), ALU.subtract, inc=cV)
            S.op("vector", "scalar_tensor_tensor", trash, fout_tile(j), 1.0,
                 fout_tile(j), ALU.mult, ALU.mult, accum_out=sts(par), inc=cV)
            S.wait("scalar", cV)
            S.op("scalar", "activation", ststd(par), sts(par), AF.Sqrt,
                 scale=c_invdm, bias=c_eps, inc=cA)
            S.wait("vector", cA)
            S.op("vector", "reciprocal", strd(par), ststd(par), inc=cV)
            S.wait("vector", cV)
            S.op("vector", "tensor_single_scalar", fout_tile(j), fout_tile(j),
                 strd(par), ALU.mult, inc=cV)
        ln2_done = cV.n
        if st == "ln2":
            for j in range(NT):
                S.wait("sync", cV, ln2_done)
                S.op("sync", "dma_start", out=out_h[s, j * 128:(j + 1) * 128, :],
                     in_=fout_tile(j), inc=cS, k=16)
            return True

        S.wait("tensor", cV, ln2_done)
        for c in range(NCK):
            for j in range(NT):
                S.op("tensor", "matmul", bk(6)[:, 16 + c:17 + c],
                     fout_tile(j)[:, c * 128:(c + 1) * 128], onesf[:, 0:1],
                     start=(j == 0), stop=(j == NT - 1),
                     inc=(cT if j == NT - 1 else None))
        S.wait("scalar", cT)
        S.op("scalar", "activation", gcb, bk(6)[:, 16:24], AF.Copy,
             scale=c_invS, inc=cA)
        S.wait("tensor", cA)
        S.wait("tensor", cW, gc1_ld)
        for oc in range(4):
            S.wait("tensor", cA, cA.n)
            for c in range(NCK):
                S.op("tensor", "matmul", bk(6)[:, 24 + oc:25 + oc],
                     wGC1(c)[:, oc * 128:(oc + 1) * 128], gcb[:, c:c + 1],
                     start=(c == 0), stop=(c == NCK - 1),
                     inc=(cT if c == NCK - 1 else None))
            S.wait("scalar", cT)
            S.op("scalar", "activation", h1gb[:, oc:oc + 1],
                 bk(6)[:, 24 + oc:25 + oc], AF.Gelu,
                 bias=gcb1[:, oc:oc + 1], inc=cA)
        S.wait("tensor", cW, smallw_ld)
        for oc in range(2):
            S.wait("tensor", cA, cA.n)
            for c in range(4):
                S.op("tensor", "matmul", bk(6)[:, 28 + oc:29 + oc],
                     gc2v(c)[:, oc * 128:(oc + 1) * 128], h1gb[:, c:c + 1],
                     start=(c == 0), stop=(c == 3),
                     inc=(cT if c == 3 else None))
            S.wait("scalar", cT)
            S.op("scalar", "activation", h2gb[:, oc:oc + 1],
                 bk(6)[:, 28 + oc:29 + oc], AF.Gelu,
                 bias=gcb2[:, oc:oc + 1], inc=cA)
        S.wait("tensor", cA, cA.n)
        S.wait("tensor", cW, smallw_setup)
        for c in range(2):
            S.op("tensor", "matmul", bk(6)[0:1, 30:31], gc3b[:, c:c + 1],
                 h2gb[:, c:c + 1], start=(c == 0), stop=(c == 1),
                 inc=(cT if c == 1 else None))
        S.wait("scalar", cT)
        S.op("scalar", "activation", gf_s, bk(6)[0:1, 30:31], AF.Sigmoid,
             bias=gcb3, inc=cA)
        S.wait("tensor", cA)
        S.op("tensor", "matmul", bk(6)[:, 31:32], onesf[0:1, 0:128], gf_s,
             start=True, stop=True, inc=cT)
        S.wait("vector", cT)
        S.op("vector", "tensor_copy", gf128, bk(6)[:, 31:32], inc=cV)
        S.wait("vector", cV)
        S.op("vector", "tensor_scalar_mul", u128, gf128, 1.4, inc=cV)
        S.wait("vector", cV)
        S.op("vector", "tensor_scalar_add", u128, u128, -1.0, inc=cV)
        S.wait("vector", cV)
        S.op("vector", "tensor_mul", B128, u128, ar128, inc=cV)
        S.wait("vector", cV)
        S.op("vector", "tensor_scalar_add", B128, B128, 0.1, inc=cV)
        gf_ready = cV.n

        S.wait("tensor", cW, cn1_ld)
        S.wait("tensor", cW, cn2_ld)
        tp_free = {0: cV.n, 1: cV.n}
        h1_free = {0: cA.n, 1: cA.n}
        h2_free = {0: cA.n, 1: cA.n}
        for cyc in range(3):
            for g in range(4):
                tg = list(range(4 * g, 4 * g + 4))
                for c in range(NCK):
                    par = c % 2
                    tp = bk(2 + par)
                    S.wait("tensor", cV, tp_free[par])
                    for t_i, j in enumerate(tg):
                        S.op("tensor", "transpose",
                             tp[:, t_i * 128:(t_i + 1) * 128],
                             fout_tile(j)[:, c * 128:(c + 1) * 128], identf,
                             inc=(cT if t_i == 3 else None))
                    S.wait("vector", cT, cT.n)
                    S.op("vector", "tensor_copy", outTg(c), tp, inc=cV)
                    tp_free[par] = cV.n
                tgp_done = cV.n
                S.wait("tensor", cV, tgp_done)
                for o in range(NCK):
                    par = o % 2
                    H1 = bk(par)
                    S.wait("tensor", cA, h1_free[par])
                    for c in range(NCK):
                        S.op("tensor", "matmul", H1,
                             wA(c)[:, o * 128:(o + 1) * 128], outTg(c),
                             start=(c == 0), stop=(c == NCK - 1),
                             inc=(cT if c == NCK - 1 else None))
                    S.wait("scalar", cT, cT.n)
                    S.op("scalar", "activation", h1b(o), H1, AF.Gelu,
                         bias=cnb1[:, o:o + 1], inc=cA)
                    h1_free[par] = cA.n
                for o in range(4):
                    par = o % 2
                    H2 = bk(4 + par)
                    S.wait("tensor", cA, h1_free[1] if o == 0 else h2_free[par])
                    if o == 0:
                        S.wait("tensor", cA, h1_free[0])
                    for c in range(NCK):
                        S.op("tensor", "matmul", H2,
                             wCN2(c)[:, o * 128:(o + 1) * 128], h1b(c),
                             start=(c == 0), stop=(c == NCK - 1),
                             inc=(cT if c == NCK - 1 else None))
                    S.wait("scalar", cT, cT.n)
                    S.op("scalar", "activation", h2b(o), H2, AF.Gelu,
                         bias=cnb2[:, o:o + 1], inc=cA)
                    h2_free[par] = cA.n
                for o in range(2):
                    par = o % 2
                    H3 = bk(par)
                    S.wait("tensor", cA, cA.n)
                    for c in range(4):
                        S.op("tensor", "matmul", H3,
                             cn3v(c)[:, o * 128:(o + 1) * 128], h2b(c),
                             start=(c == 0), stop=(c == 3),
                             inc=(cT if c == 3 else None))
                    S.wait("scalar", cT, cT.n)
                    S.op("scalar", "activation", h3b(o), H3, AF.Gelu,
                         bias=cnb3[:, o:o + 1], inc=cA)
                    h1_free[par] = cA.n
                S.wait("tensor", cA, cA.n)
                if cyc == 0 and g == 0:
                    S.wait("tensor", cV, gf_ready)
                for t_i, j in enumerate(tg):
                    for c in range(2):
                        S.op("tensor", "matmul", bk(6)[:, j:j + 1],
                             h3b(c)[:, t_i * 128:(t_i + 1) * 128],
                             cn4b[:, c:c + 1], start=(c == 0), stop=(c == 1),
                             inc=(cT if c == 1 else None))
            S.wait("scalar", cT)
            S.op("scalar", "activation", tfs, bk(6)[:, 0:16], AF.Sigmoid,
                 bias=cnb4c, inc=cA)
            S.wait("vector", cA)
            if cyc == 0:
                S.wait("vector", cV, gf_ready)
            for j in range(NT):
                S.op("vector", "scalar_tensor_tensor", coup[:, j:j + 1],
                     tfs[:, j:j + 1], A128, B128, ALU.mult, ALU.add,
                     inc=(cV if j == NT - 1 else None))
            S.wait("vector", cV)
            S.op("vector", "tensor_scalar_mul", ncoup, coup, -1.0, inc=cV)
            S.wait("vector", cV)
            gp_done = {0: cG.n, 1: cG.n}
            for j in range(NT):
                par = j % 2
                ft = fout_tile(j)
                x0 = ft[:, 0:256]; x1 = ft[:, 256:512]
                x2 = ft[:, 512:768]; x3 = ft[:, 768:1024]
                cj = coup[:, j:j + 1]
                nj = ncoup[:, j:j + 1]
                S.wait("vector", cG, gp_done[par])
                S.op("vector", "scalar_tensor_tensor", tw0(par), x2, cj, x0,
                     ALU.mult, ALU.add)
                S.op("vector", "scalar_tensor_tensor", tw1(par), x3, nj, x1,
                     ALU.mult, ALU.add)
                S.op("vector", "scalar_tensor_tensor", x2, x0, nj, x2,
                     ALU.mult, ALU.add)
                S.op("vector", "scalar_tensor_tensor", x3, x1, cj, x3,
                     ALU.mult, ALU.add, inc=cV)
                S.wait("gpsimd", cV, cV.n)
                S.op("gpsimd", "tensor_copy", x0, tw0(par))
                S.op("gpsimd", "tensor_copy", x1, tw1(par), inc=cG)
                gp_done[par] = cG.n
            S.wait("tensor", cG, cG.n)
            S.wait("tensor", cV, cV.n)
            if st == f"cyc{cyc + 1}":
                done_v, done_g = cV.n, cG.n
                for j in range(NT):
                    S.wait("sync", cV, done_v)
                    S.wait("sync", cG, done_g)
                    S.op("sync", "dma_start",
                         out=out_h[s, j * 128:(j + 1) * 128, :],
                         in_=fout_tile(j), inc=cS, k=16)
                return True

        S.wait("vector", cG, cG.n)
        st_done = {0: cS.n, 1: cS.n}
        for j in range(NT):
            par = j % 2
            S.wait("vector", cS, st_done[par])
            S.op("vector", "tensor_add", stg(par), xw_tile(j), fout_tile(j),
                 inc=cV)
            S.wait("sync", cV, cV.n)
            S.op("sync", "dma_start", out=out_h[s, j * 128:(j + 1) * 128, :],
                 in_=stg(par), inc=cS, k=16)
            st_done[par] = cS.n

    for s in range(2):
        if seq_program(s):
            break

    S.wait("sync", cS, cS.n)

    with es, nc.Block() as block:
        @block.sync
        def _(e):
            for fn in S.ev["sync"]:
                fn(e)

        @block.gpsimd
        def _(e):
            for fn in S.ev["gpsimd"]:
                fn(e)

        @block.vector
        def _(e):
            for fn in S.ev["vector"]:
                fn(e)

        @block.scalar
        def _(e):
            for fn in S.ev["scalar"]:
                fn(e)

        @block.tensor
        def _(e):
            for fn in S.ev["tensor"]:
                fn(e)
    return nc


def _prep_in_maps(inputs):
    x = np.ascontiguousarray(np.asarray(inputs["L0_attn_x"], dtype=np.float32))
    f32 = lambda a: np.ascontiguousarray(np.asarray(a, dtype=np.float32))
    shared = {
        "w_q": f32(inputs["L0_attn_w_q"]),
        "w_kv": f32(inputs["L0_attn_w_kv"]),
        "w_out": f32(inputs["L0_attn_w_out"]),
        "b_out": f32(inputs["L0_attn_b_out"]).reshape(1, DM),
        "scale": f32(inputs["L0_attn_scale"]).reshape(1, NH),
        "cn_w1": f32(inputs["L0_ffn_cn_w1"]),
        "cn_w2": f32(inputs["L0_ffn_cn_w2"]),
        "cn_w3": f32(inputs["L0_ffn_cn_w3"]),
        "cn_w4": f32(inputs["L0_ffn_cn_w4"]).reshape(256, 1),
        "gc_w1": f32(inputs["L0_ffn_gc_w1"]),
        "gc_w2": f32(inputs["L0_ffn_gc_w2"]),
        "gc_w3": f32(inputs["L0_ffn_gc_w3"]).reshape(256, 1),
        "cn_b1": f32(inputs["L0_ffn_cn_b1"]).reshape(8, 128, 1),
        "cn_b2": f32(inputs["L0_ffn_cn_b2"]).reshape(4, 128, 1),
        "cn_b3": f32(inputs["L0_ffn_cn_b3"]).reshape(2, 128, 1),
        "gc_b1": f32(inputs["L0_ffn_gc_b1"]).reshape(4, 128, 1),
        "gc_b2": f32(inputs["L0_ffn_gc_b2"]).reshape(2, 128, 1),
        "cn_b4": f32(inputs["L0_ffn_cn_b4"]).reshape(1, 1),
        "gc_b3": f32(inputs["L0_ffn_gc_b3"]).reshape(1, 1),
        "ar": f32(inputs["L0_ffn_adaptive_range"]).reshape(1, 1),
        "onesf": np.ones((128, 128), dtype=np.float32),
        "identf": np.eye(128, dtype=np.float32),
        "identb": np.eye(128, dtype=np.float32).astype(ml_dtypes.bfloat16),
        "onesb": np.ones((128, 128), dtype=ml_dtypes.bfloat16),
    }
    in_maps = []
    for core in range(8):
        m = dict(shared)
        m["x"] = np.ascontiguousarray(x[2 * core:2 * core + 2])
        in_maps.append(m)
    return in_maps


def kernel(**inputs):
    in_maps = _prep_in_maps(inputs)
    nc = build_nc("full")
    res = run_bass_kernel_spmd(nc, in_maps, core_ids=list(range(8)))
    out = np.concatenate([np.asarray(r["out"], dtype=np.float32)
                          for r in res.results], axis=0)
    return out.reshape(16, SEQ, DM)
